# version 13
# baseline (speedup 1.0000x reference)
import numpy as np
from contextlib import ExitStack

import concourse.bass as bass
import concourse.mybir as mybir
from concourse.bass_utils import run_bass_kernel_spmd

F32 = mybir.dt.float32
BF16 = mybir.dt.bfloat16
AF = mybir.ActivationFunctionType
ALU = mybir.AluOpType

D = 1024
NH, DH, DLAT = 16, 64, 256
E, HFF = 8, 2816
NCORES = 8
TPC = 2048
TAU = 512
NTAU = TPC // TAU
NHB = HFF // 128
BIG = 1.0e5
EPS = 1e-5
NB12 = E * NHB * 2
NB3 = E * 8 * 2


class Res:
    __slots__ = ("p", "c")

    def __init__(self):
        self.p = None
        self.c = []

    def rdeps(self):
        return [self.p] if self.p else []

    def wdeps(self):
        d = list(self.c)
        if self.p:
            d.append(self.p)
        return d

    def produced(self, ev):
        if ev is not None:
            self.p = ev
            self.c = []

    def consumed(self, ev):
        if ev is not None:
            self.c.append(ev)


class Walk:

    def __init__(self, role, eng, sems, owner):
        self.role = role
        self.eng = eng
        self.sems = sems
        self.owner = owner
        self.ctr = {k: 0 for k in sems}
        self.last_wait = {k: 0 for k in sems}

    def op(self, engine, emit, deps=(), sig=None, sdeps=()):
        ev = None
        if sig is not None:
            amt = 16 if self.owner[sig] == "dma" else 1
            self.ctr[sig] += amt
            ev = (sig, self.ctr[sig])
        if self.role == engine:
            for d in deps:
                if d is None:
                    continue
                s, v = d
                if v <= 0 or self.owner[s] == self.role:
                    continue
                if v <= self.last_wait[s]:
                    continue
                self.eng.wait_ge(self.sems[s], v)
                self.last_wait[s] = v
            for d in sdeps:
                if d is None:
                    continue
                s, v = d
                if v <= 0 or v <= self.last_wait[s]:
                    continue
                self.eng.wait_ge(self.sems[s], v)
                self.last_wait[s] = v
            inst = emit(self.eng)
            if sig is not None:
                inst.then_inc(self.sems[sig], 16 if self.owner[sig] == "dma" else 1)
        return ev

    def raw_wait(self, engine, sem, val):
        if self.role == engine and val > self.last_wait[sem]:
            self.eng.wait_ge(self.sems[sem], val)
            self.last_wait[sem] = val


def build_nc(ntau=NTAU, debug=False):
    nc = bass.Bass("TRN2", target_bir_lowering=False, detect_race_conditions=False)

    def din(name, shape, dtype=F32):
        return nc.dram_tensor(name, shape, dtype, kind="ExternalInput").ap()

    xT = din("xT", [D, TPC])
    tabC = din("tabC", [128, TPC])
    tabS = din("tabS", [128, TPC])
    wq = din("wq", [D, D])
    wdn = din("wdn", [D, DLAT])
    wup = din("wup", [DLAT, 2 * D])
    wout = din("wout", [D, D])
    wgT = din("wgT", [D, E])
    w1 = din("w1", [E, D, HFF])
    w2 = din("w2", [E, D, HFF])
    w3 = din("w3", [E, HFF, D])
    biasp = din("biasp", [128, 34])
    maskC = din("maskC", [9, 128])
    maskD = din("maskD", [9, 128])
    identD = din("identD", [128, 128])
    onesrow = din("onesrow", [1, 128])
    sel8 = din("sel8", [8, 1024])
    outT = nc.dram_tensor("outT", [D, TPC], F32, kind="ExternalOutput").ap()
    DBG = {}
    if debug:
        for nm, shp in [("dbg_x", [128, 4096]), ("dbg_lat", [128, 512]),
                        ("dbg_q", [128, 2048]), ("dbg_kv", [128, 4096]),
                        ("dbg_mQ", [64, 2048]), ("dbg_mK", [64, 2048]), ("dbg_mV", [64, 2048]),
                        ("dbg_p", [128, 128]), ("dbg_att", [128, 4096]),
                        ("dbg_sx", [128, 4096]), ("dbg_x1", [128, 4096]),
                        ("dbg_g8", [8, 512]), ("dbg_w8", [8, 512]),
                        ("dbg_gt", [128, 32]), ("dbg_gm", [128, 8]),
                        ("dbg_gwt", [128, 32]), ("dbg_gs2", [128, 32]),
                        ("dbg_acc", [128, 4096])]:
            DBG[nm] = nc.dram_tensor(nm, shp, F32, kind="ExternalOutput").ap()

    s12h = nc.dram_tensor("s12h", [NB12, 128, 1024], BF16).ap()
    s12l = nc.dram_tensor("s12l", [NB12, 128, 1024], BF16).ap()
    s3h = nc.dram_tensor("s3h", [NB3, 128, 1408], BF16).ap()
    s3l = nc.dram_tensor("s3l", [NB3, 128, 1408], BF16).ap()

    es = ExitStack()
    SB = es.enter_context

    c_ident = SB(nc.sbuf_tensor("c_ident", [128, 128], F32))
    c_onesrow = SB(nc.sbuf_tensor("c_onesrow", [1, 128], F32))
    c_onescol = SB(nc.sbuf_tensor("c_onescol", [128, 1], F32))
    c_sel = SB(nc.sbuf_tensor("c_sel", [8, 1024], F32))
    c_maskC = SB(nc.sbuf_tensor("c_maskC", [9, 128], F32))
    c_maskD = SB(nc.sbuf_tensor("c_maskD", [9, 128], F32))
    c_bias = SB(nc.sbuf_tensor("c_bias", [128, 34], F32))
    c_wg_f = SB(nc.sbuf_tensor("c_wg_f", [128, 64], F32))
    T_tab = SB(nc.sbuf_tensor("T_tab", [128, 512], F32))
    A_x = SB(nc.sbuf_tensor("A_x", [128, 4096], F32))
    A_attn = SB(nc.sbuf_tensor("A_attn", [128, 4096], F32))
    A_h = SB(nc.sbuf_tensor("A_h", [128, 4096], BF16))
    A_l = SB(nc.sbuf_tensor("A_l", [128, 4096], BF16))
    A_q = SB(nc.sbuf_tensor("A_q", [128, 2048], F32))
    A_kv = SB(nc.sbuf_tensor("A_kv", [128, 4096], F32))
    A_lat = SB(nc.sbuf_tensor("A_lat", [128, 512], F32))
    R_tmp = SB(nc.sbuf_tensor("R_tmp", [128, 256], F32))
    R2 = SB(nc.sbuf_tensor("R2", [128, 256], F32))
    G_s = SB(nc.sbuf_tensor("G_s", [128, 512], F32))
    m_Q = SB(nc.sbuf_tensor("m_Q", [64, 2048], F32))
    m_K = SB(nc.sbuf_tensor("m_K", [64, 2048], F32))
    m_V = SB(nc.sbuf_tensor("m_V", [64, 2048], F32))
    p_hat = [SB(nc.sbuf_tensor(f"p_hat{i}", [128, 128], F32)) for i in range(2)]
    v_hat = [SB(nc.sbuf_tensor(f"v_hat{i}", [128, 65], F32)) for i in range(2)]
    t_av = [SB(nc.sbuf_tensor(f"t_av{i}", [128, 64], F32)) for i in range(2)]
    r_rec = SB(nc.sbuf_tensor("r_rec", [128, 2], F32))
    S_scal = SB(nc.sbuf_tensor("S_scal", [1, 2048], F32))
    G8 = SB(nc.sbuf_tensor("G8", [8, 512], F32))
    Gt = SB(nc.sbuf_tensor("Gt", [128, 32], F32))
    Gwt = SB(nc.sbuf_tensor("Gwt", [128, 32], F32))
    Gs2t = SB(nc.sbuf_tensor("Gs2t", [128, 32], F32))
    Gm = SB(nc.sbuf_tensor("Gm", [128, 8], F32))
    G_w8 = SB(nc.sbuf_tensor("G_w8", [8, 512], F32))
    wB_s = SB(nc.sbuf_tensor("wB_s", [128, 512], F32))
    gu_h = SB(nc.sbuf_tensor("gu_h", [128, NHB * 512], BF16))
    gu_l = SB(nc.sbuf_tensor("gu_l", [128, NHB * 512], BF16))
    Wf = SB(nc.sbuf_tensor("Wf", [128, 2048], F32))
    Wg_h = [[SB(nc.sbuf_tensor(f"Wgh{g}{s}", [128, 1024], BF16)) for s in range(2)] for g in range(2)]
    Wg_l = [[SB(nc.sbuf_tensor(f"Wgl{g}{s}", [128, 1024], BF16)) for s in range(2)] for g in range(2)]
    V3_h = [SB(nc.sbuf_tensor(f"V3h{g}", [128, 1408], BF16)) for g in range(2)]
    V3_l = [SB(nc.sbuf_tensor(f"V3l{g}", [128, 1408], BF16)) for g in range(2)]

    pb = [es.enter_context(nc.psum_tensor(f"pb{i}", [128, 512], F32)) for i in range(8)]

    sem_owner = {
        "dX": "dma", "dC": "dma", "dT": "dma",
        "dWa": "dma", "dWb": "dma", "dVa": "dma", "dVb": "dma",
        "dAa": "dma", "dAb": "dma",
        "dScr": "dma", "dOut": "dma",
        "sT": "T", "sA": "A", "sV": "V",
    }
    sems = {k: es.enter_context(nc.semaphore(k)) for k in sem_owner}
    block = es.enter_context(nc.Block())

    def wq_blk(m):
        return wq[:, m * 128:(m + 1) * 128].rearrange("(kd p) m -> p kd m", p=128)

    def wdn_blk(m):
        return wdn[:, m * 128:(m + 1) * 128].rearrange("(kd p) m -> p kd m", p=128)

    def wup_blk(b):
        return wup[:, b * 512:(b + 1) * 512].rearrange("(kl p) m -> p kl m", p=128)

    def wout_blk(m):
        return wout[:, m * 128:(m + 1) * 128].rearrange("(kd p) m -> p kd m", p=128)

    def w12_blk(t, e, hb):
        src = w1 if t == 0 else w2
        return src[e, :, hb * 128:(hb + 1) * 128].rearrange("(kd p) m -> p kd m", p=128)

    bid = 0
    units = []
    for e in range(E):
        for hb in range(NHB):
            units.append(([(w12_blk(0, e, hb), bid, 0), (w12_blk(1, e, hb), bid + 1, 1)], True))
            bid += 2
    assert bid == NB12, bid
    NUNITS = len(units)

    att_blocks = []
    for _ in range(2):
        for m in range(8):
            att_blocks.append(wq_blk(m))
        for m in range(2):
            att_blocks.append(wdn_blk(m))
        for b in range(4):
            att_blocks.append(wup_blk(b))
    for m in range(8):
        att_blocks.append(wout_blk(m))
    NATT = len(att_blocks)

    def w3_blk(k):
        e_, r = divmod(k, 16)
        dc, half = divmod(r, 2)
        return w3[e_, half * 1408:(half + 1) * 1408, dc * 128:(dc + 1) * 128].rearrange(
            "(kh p) m -> p kh m", p=128)

    def walk(role, eng):
        w = Walk(role, eng, sems, sem_owner)

        r_banks = [Res() for _ in range(8)]
        r_Ax, r_Aattn, r_Ah, r_Al = Res(), Res(), Res(), Res()
        r_Aq, r_Akv, r_lat = Res(), Res(), Res()
        r_tab, r_Rtmp, r_R2, r_Gs = Res(), Res(), Res(), Res()
        r_mQ, r_mK, r_mV = Res(), Res(), Res()
        r_ph = [Res(), Res()]
        r_vh = [Res(), Res()]
        r_tav = [Res(), Res()]
        r_rr = [Res(), Res()]
        r_Ss, r_G8, r_Gtok, r_Gw8, r_wB, r_gu = Res(), Res(), Res(), Res(), Res(), Res()
        r_Wf = [Res(), Res()]
        r_W12 = [Res(), Res()]
        r_V3 = [Res(), Res()]

        def dma(emitfn, deps=(), sig="dX"):
            return w.op("Y", emitfn, deps=deps, sig=sig)

        def dump(nm, buf, res):
            if not debug:
                return
            ev = dma(lambda e: e.dma_start(out=DBG[nm][:, :], in_=buf), deps=res.rdeps(),
                     sig="dOut")
            res.consumed(ev)

        dma(lambda e: e.dma_start(out=c_ident[:, :], in_=identD[:, :]), sig="dC")
        dma(lambda e: e.dma_start(out=c_onesrow[0:1, :], in_=onesrow[0:1, :]), sig="dC")
        dma(lambda e: e.dma_start(out=c_maskC[:, :], in_=maskC[:, :]), sig="dC")
        dma(lambda e: e.dma_start(out=c_maskD[:, :], in_=maskD[:, :]), sig="dC")
        dma(lambda e: e.dma_start(out=c_bias[:, :], in_=biasp[:, :]), sig="dC")
        dma(lambda e: e.dma_start(out=c_sel[:, :], in_=sel8[:, :]), sig="dC")
        ev_const = dma(lambda e: e.dma_start(
            out=c_wg_f.ap().rearrange("p (kd e) -> p kd e", kd=8, e=8),
            in_=wgT.rearrange("(kd p) e -> p kd e", p=128)), sig="dC")
        ev_oc = w.op("V", lambda e: e.memset(c_onescol[:, :], 1.0), sig="sV")
        r_vh[0].produced(w.op("V", lambda e: e.memset(v_hat[0][:, 64:65], 1.0), sig="sV"))
        r_vh[1].produced(w.op("V", lambda e: e.memset(v_hat[1][:, 64:65], 1.0), sig="sV"))

        st = {"next": 0}
        scr_ev = {}

        def load_unit(k):
            g = k % 2
            dsem = "dWa" if g == 0 else "dWb"
            srcs, ws = units[k % NUNITS]
            if k < NUNITS:
                last = None
                for (src_ap, blk, slot) in srcs:
                    fres = r_Wf[slot]
                    fdst = Wf[:, slot * 1024:(slot + 1) * 1024]
                    a, b = src_ap.shape[1], src_ap.shape[2]
                    ev_d = dma(lambda e, src_ap=src_ap, fdst=fdst, a=a, b=b: e.dma_start(
                        out=fdst.rearrange("p (a b) -> p a b", a=a, b=b), in_=src_ap),
                        deps=fres.wdeps(), sig=dsem)
                    fres.produced(ev_d)
                    hdst, ldst = Wg_h[g][slot], Wg_l[g][slot]
                    gres = r_W12[g]
                    ev_h = w.op("A", lambda e, hdst=hdst, fdst=fdst: e.activation(
                        hdst[:, :], fdst, AF.Copy),
                        deps=fres.rdeps() + gres.wdeps(), sig="sA")
                    ev_l = w.op("V", lambda e, ldst=ldst, fdst=fdst, hdst=hdst: e.tensor_tensor(
                        ldst[:, :], fdst, hdst[:, :], op=ALU.subtract),
                        deps=fres.rdeps() + [ev_h], sig="sV")
                    fres.consumed(ev_l)
                    gres.produced(ev_l)
                    if ws:
                        ev_s1 = dma(lambda e, blk=blk, hdst=hdst: e.dma_start(
                            out=s12h[blk], in_=hdst[:, :]), deps=[ev_h], sig="dScr")
                        ev_s2 = dma(lambda e, blk=blk, ldst=ldst: e.dma_start(
                            out=s12l[blk], in_=ldst[:, :]), deps=[ev_l], sig="dScr")
                        gres.consumed(ev_s1)
                        gres.consumed(ev_s2)
                        scr_ev[blk] = ev_s2
                    last = ev_l
                return last
            else:
                gres = r_W12[g]
                deps = gres.wdeps() + [("dScr", w.ctr["dScr"])]
                last = None
                for (src_ap, blk, slot) in srcs:
                    dma(lambda e, blk=blk, g=g, slot=slot: e.dma_start(
                        out=Wg_h[g][slot][:, :], in_=s12h[blk]), deps=deps, sig=dsem)
                    last = dma(lambda e, blk=blk, g=g, slot=slot: e.dma_start(
                        out=Wg_l[g][slot][:, :], in_=s12l[blk]), deps=deps, sig=dsem)
                gres.produced(last)
                return last

        NTOT = NUNITS * ntau

        def consume_unit():
            k = st["next"]
            st["next"] += 1
            if k == 0:
                load_unit(0)
            if st["next"] < NTOT:
                load_unit(st["next"])
            g = k % 2
            return g, r_W12[g].rdeps()

        st3 = {"next": 0}
        scr3_ev = {}

        def load_unit3(k):
            g = k % 2
            dsem = "dVa" if g == 0 else "dVb"
            if k < NB3:
                ev_d = dma(lambda e, k=k: e.dma_start(
                    out=Wf[:, 0:1408].rearrange("p (kh m) -> p kh m", kh=11, m=128),
                    in_=w3_blk(k)),
                    deps=r_Wf[0].wdeps() + r_Wf[1].wdeps(), sig=dsem)
                r_Wf[0].produced(ev_d)
                r_Wf[1].produced(ev_d)
                gres = r_V3[g]
                ev_h = w.op("A", lambda e, g=g: e.activation(V3_h[g][:, :], Wf[:, 0:1408], AF.Copy),
                            deps=[ev_d] + gres.wdeps(), sig="sA")
                ev_l = w.op("V", lambda e, g=g: e.tensor_tensor(
                    V3_l[g][:, :], Wf[:, 0:1408], V3_h[g][:, :], op=ALU.subtract),
                    deps=[ev_d, ev_h], sig="sV")
                r_Wf[0].consumed(ev_l)
                r_Wf[1].consumed(ev_l)
                gres.produced(ev_l)
                ev_s1 = dma(lambda e, k=k, g=g: e.dma_start(out=s3h[k], in_=V3_h[g][:, :]),
                            deps=[ev_h], sig="dScr")
                ev_s2 = dma(lambda e, k=k, g=g: e.dma_start(out=s3l[k], in_=V3_l[g][:, :]),
                            deps=[ev_l], sig="dScr")
                gres.consumed(ev_s1)
                gres.consumed(ev_s2)
                scr3_ev[k] = ev_s2
            else:
                kb = k % NB3
                gres = r_V3[g]
                deps = gres.wdeps() + [("dScr", w.ctr["dScr"])]
                dma(lambda e, kb=kb, g=g: e.dma_start(out=V3_h[g][:, :], in_=s3h[kb]),
                    deps=deps, sig=dsem)
                last = dma(lambda e, kb=kb, g=g: e.dma_start(out=V3_l[g][:, :], in_=s3l[kb]),
                           deps=deps, sig=dsem)
                gres.produced(last)

        N3TOT = NB3 * ntau

        def consume_unit3():
            k = st3["next"]
            st3["next"] += 1
            if k == 0:
                load_unit3(0)
            if st3["next"] < N3TOT:
                load_unit3(st3["next"])
            g = k % 2
            return g, r_V3[g].rdeps()

        ast = {"next": 0}
        NATOT = NATT * ntau

        def load_att(k):
            g = k % 2
            dsem = "dAa" if g == 0 else "dAb"
            blk = att_blocks[k % NATT]
            a_, b_ = blk.shape[1], blk.shape[2]
            ev_d = dma(lambda e, blk=blk, g=g, a_=a_, b_=b_: e.dma_start(
                out=Wf[:, g * 1024:(g + 1) * 1024].rearrange("p (a b) -> p a b", a=a_, b=b_),
                in_=blk), deps=r_Wf[g].wdeps(), sig=dsem)
            r_Wf[g].produced(ev_d)

        def consume_att():
            k = ast["next"]
            ast["next"] += 1
            if k % NATT == 0:
                load_att(k)
            nxt = ast["next"]
            if nxt < NATOT and nxt % NATT != 0:
                load_att(nxt)
            g = k % 2
            return g, r_Wf[g].rdeps()

        def mm3(out_ap, lhs_h_fn, lhs_l_fn, rhs_h_fn, rhs_l_fn, nk, deps_first, sig_last=True,
                start=True, stop=True):
            ev = None
            for kd in range(nk):
                lh, ll = lhs_h_fn(kd), lhs_l_fn(kd)
                rh, rl = rhs_h_fn(kd), rhs_l_fn(kd)
                s0 = start and kd == 0
                dep = deps_first if kd == 0 else []
                w.op("T", lambda e, lh=lh, rh=rh, s0=s0: e.matmul(out_ap, lh, rh, start=s0, stop=False),
                     deps=dep)
                w.op("T", lambda e, lh=lh, rl=rl: e.matmul(out_ap, lh, rl, start=False, stop=False))
                is_last = stop and kd == nk - 1
                ev = w.op("T", lambda e, ll=ll, rh=rh, is_last=is_last: e.matmul(
                    out_ap, ll, rh, start=False, stop=is_last),
                    sig=("sT" if (kd == nk - 1 and sig_last) else None))
            return ev

        def split(src_sb, src_res, h_sb, h_res, l_sb, l_res, ncol, extra_deps=()):
            ev_h = w.op("A", lambda e: e.activation(h_sb[:, 0:ncol], src_sb[:, 0:ncol], AF.Copy),
                        deps=src_res.rdeps() + h_res.wdeps() + list(extra_deps), sig="sA")
            h_res.produced(ev_h)
            ev_l = w.op("V", lambda e: e.tensor_tensor(l_sb[:, 0:ncol], src_sb[:, 0:ncol],
                                                       h_sb[:, 0:ncol], op=ALU.subtract),
                        deps=src_res.rdeps() + [ev_h] + l_res.wdeps(), sig="sV")
            l_res.produced(ev_l)
            src_res.consumed(ev_l)
            h_res.consumed(ev_l)

        def layernorm(src_sb, src_res, out_sb, out_res):
            b6, b7 = r_banks[6], r_banks[7]
            ev_sum = None
            for c in range(8):
                s0 = c == 0
                dep = (src_res.rdeps() + b7.wdeps() + [ev_oc]) if s0 else []
                ev_sum = w.op("T", lambda e, c=c, s0=s0: e.matmul(
                    pb[7][0:1, :], c_onescol[:, 0:1], src_sb[:, c * 512:(c + 1) * 512],
                    start=s0, stop=(c == 7)), deps=dep, sig=("sT" if c == 7 else None))
            b7.produced(ev_sum)
            ev_sq = None
            for c in range(8):
                ev_s = w.op("A", lambda e, c=c: e.activation(
                    G_s[:, :], src_sb[:, c * 512:(c + 1) * 512], AF.Square),
                    deps=src_res.rdeps() + r_Gs.wdeps(), sig="sA")
                r_Gs.produced(ev_s)
                s0 = c == 0
                dep = [ev_s, ev_oc] + (b6.wdeps() if s0 else [])
                ev_sq = w.op("T", lambda e, s0=s0, c=c: e.matmul(
                    pb[6][0:1, :], c_onescol[:, 0:1], G_s[:, :],
                    start=s0, stop=(c == 7)), deps=dep, sig="sT")
                r_Gs.consumed(ev_sq)
            b6.produced(ev_sq)
            SA = S_scal[0:1, 0:512]
            SBb = S_scal[0:1, 512:1024]
            SC = S_scal[0:1, 1024:1536]
            SD = S_scal[0:1, 1536:2048]
            ev = w.op("V", lambda e: e.tensor_single_scalar(
                SA, pb[7][0:1, :], 1.0 / D, op=ALU.mult),
                deps=b7.rdeps() + r_Ss.wdeps(), sig="sV")
            b7.consumed(ev)
            e1 = ev
            ev = w.op("V", lambda e: e.tensor_single_scalar(
                SBb, pb[6][0:1, :], 1.0 / D, op=ALU.mult),
                deps=b6.rdeps(), sig="sV", sdeps=[ev])
            b6.consumed(ev)
            ev = w.op("V", lambda e: e.tensor_tensor(SC, SA, SA, op=ALU.mult),
                      sig="sV", sdeps=[ev])
            ev_mub = w.op("T", lambda e: e.matmul(pb[6][:, :], c_onesrow[0:1, :], SA,
                                                  start=True, stop=True),
                          deps=[ev, ev_const] + b6.wdeps(), sig="sT")
            b6.produced(ev_mub)
            ev = w.op("V", lambda e: e.tensor_tensor(SD, SBb, SC, op=ALU.subtract),
                      sig="sV", sdeps=[ev])
            ev = w.op("V", lambda e: e.tensor_single_scalar(SC, SD, EPS, op=ALU.add),
                      sig="sV", sdeps=[ev])
            ev = w.op("A", lambda e: e.activation(SBb, SC, AF.Sqrt), deps=[ev], sig="sA")
            ev = w.op("V", lambda e: e.reciprocal(SD, SBb), deps=[ev], sig="sV")
            ev = w.op("V", lambda e: e.tensor_tensor(SA, SD, SD, op=ALU.mult),
                      deps=[ev_mub], sig="sV", sdeps=[ev])
            ev = w.op("V", lambda e: e.tensor_tensor(SBb, SA, SC, op=ALU.mult),
                      sig="sV", sdeps=[ev])
            ev = w.op("V", lambda e: e.tensor_scalar(SA, SBb, -0.5, 1.5,
                                                     op0=ALU.mult, op1=ALU.add),
                      sig="sV", sdeps=[ev])
            ev = w.op("V", lambda e: e.tensor_tensor(SBb, SD, SA, op=ALU.mult),
                      sig="sV", sdeps=[ev])
            ev = w.op("V", lambda e: e.tensor_tensor(SA, SBb, SBb, op=ALU.mult),
                      sig="sV", sdeps=[ev])
            ev = w.op("V", lambda e: e.tensor_tensor(SD, SA, SC, op=ALU.mult),
                      sig="sV", sdeps=[ev])
            ev = w.op("V", lambda e: e.tensor_scalar(SA, SD, -0.5, 1.5,
                                                     op0=ALU.mult, op1=ALU.add),
                      sig="sV", sdeps=[ev])
            ev_n = w.op("V", lambda e: e.tensor_tensor(SD, SBb, SA, op=ALU.mult),
                        sig="sV", sdeps=[ev])
            ev_rsb = w.op("T", lambda e: e.matmul(pb[7][:, :], c_onesrow[0:1, :], SD,
                                                  start=True, stop=True),
                          deps=[ev_n] + b7.wdeps(), sig="sT")
            b7.produced(ev_rsb)
            r_Ss.produced(ev_n)
            r_Ss.consumed(ev_rsb)
            r_Ss.consumed(ev_mub)
            ev_last = None
            for c in range(8):
                d1 = w.op("V", lambda e, c=c: e.tensor_tensor(
                    G_s[:, :], src_sb[:, c * 512:(c + 1) * 512],
                    pb[6][:, :], op=ALU.subtract),
                    deps=b6.rdeps() + src_res.rdeps() + r_Gs.wdeps(), sig="sV")
                r_Gs.produced(d1)
                ev_last = w.op("V", lambda e, c=c: e.tensor_tensor(
                    out_sb[:, c * 512:(c + 1) * 512], G_s[:, :],
                    pb[7][:, :], op=ALU.mult),
                    deps=b7.rdeps() + [d1] + (out_res.wdeps() if c == 0 else []), sig="sV")
                r_Gs.consumed(ev_last)
                b6.consumed(ev_last)
                b7.consumed(ev_last)
            src_res.consumed(ev_last)
            out_res.produced(ev_last)

        def rope_chunk(buf, res, col0, tab_off, krows):
            cos = T_tab[:, 0:256]
            sin = T_tab[:, 256:512]
            sw = None
            nblk = krows // 64
            for b in range(nblk):
                w.op("V", lambda e, b=b: e.tensor_copy(
                    R_tmp[b * 64:b * 64 + 32, :], buf[b * 64 + 32:b * 64 + 64, col0:col0 + 256]),
                    deps=res.rdeps() + r_Rtmp.wdeps() + r_tab.rdeps())
                sw = w.op("V", lambda e, b=b: e.tensor_copy(
                    R_tmp[b * 64 + 32:b * 64 + 64, :], buf[b * 64:b * 64 + 32, col0:col0 + 256]),
                    sig="sV")
            r_Rtmp.produced(sw)
            e1 = w.op("V", lambda e: e.tensor_tensor(
                R2[0:krows, :], R_tmp[0:krows, :], sin[0:krows, :], op=ALU.mult),
                deps=[sw] + r_tab.rdeps() + r_R2.wdeps(), sig="sV")
            e2 = w.op("V", lambda e: e.tensor_tensor(
                R_tmp[0:krows, :], buf[0:krows, col0:col0 + 256],
                cos[0:krows, :], op=ALU.mult), deps=[sw], sig="sV")
            e3 = w.op("V", lambda e: e.tensor_tensor(
                buf[0:krows, col0:col0 + 256], R_tmp[0:krows, :],
                R2[0:krows, :], op=ALU.add), sig="sV")
            r_Rtmp.consumed(e3)
            r_R2.produced(e1)
            r_R2.consumed(e3)
            r_tab.consumed(e3)
            res.produced(e3)


        ev_store_prev = None

        for it in range(ntau):

            xdeps = r_Ax.wdeps() + r_tab.wdeps()
            if ev_store_prev is not None:
                xdeps.append(ev_store_prev)
            dma(lambda e, it=it: e.dma_start(
                out=A_x.ap().rearrange("p (c t) -> p c t", c=8, t=512),
                in_=xT[:, it * 512:(it + 1) * 512].rearrange("(c p) t -> p c t", p=128)),
                deps=xdeps, sig="dX")
            r_Ax.produced(("dX", w.ctr["dX"]))

            if it == 0:
                dump("dbg_x", A_x[:, :], r_Ax)
            split(A_x, r_Ax, A_h, r_Ah, A_l, r_Al, 4096)

            for ha in range(2):
                toff = ha * 256
                dma(lambda e, it=it, ha=ha: e.dma_start(
                    out=T_tab[:, 0:256],
                    in_=tabC[:, it * 512 + ha * 256:it * 512 + ha * 256 + 256]),
                    deps=r_tab.wdeps(), sig="dT")
                ev_tb = dma(lambda e, it=it, ha=ha: e.dma_start(
                    out=T_tab[:, 256:512],
                    in_=tabS[:, it * 512 + ha * 256:it * 512 + ha * 256 + 256]), sig="dT")
                r_tab.produced(ev_tb)

                def proj_block(mf, outbuf, out_res, outcol, bias_col, nk, rhs, rhs_res,
                               rhs_off):
                    g, wdeps = consume_att()
                    bank = r_banks[mf % 2]
                    pbx = pb[mf % 2]
                    ev_mm = None
                    for kd in range(nk):
                        dep = (wdeps + bank.wdeps() + rhs_res.rdeps()) if kd == 0 else []
                        ev_mm = w.op("T", lambda e, g=g, kd=kd, s0=(kd == 0),
                                     s1=(kd == nk - 1): e.matmul(
                            pbx[:, 0:256], Wf[:, g * 1024 + kd * 128:g * 1024 + (kd + 1) * 128],
                            rhs[:, rhs_off(kd):rhs_off(kd) + 256], start=s0, stop=s1),
                            deps=dep, sig=("sT" if kd == nk - 1 else None))
                    bank.produced(ev_mm)
                    r_Wf[g].consumed(ev_mm)
                    rhs_res.consumed(ev_mm)
                    ev_ev = w.op("A", lambda e: e.activation(
                        outbuf[:, outcol:outcol + 256], pbx[:, 0:256], AF.Identity,
                        bias=c_bias[:, bias_col:bias_col + 1]),
                        deps=bank.rdeps() + out_res.wdeps() + [ev_const], sig="sA")
                    bank.consumed(ev_ev)
                    out_res.produced(ev_ev)

                for mf in range(8):
                    proj_block(mf, A_q, r_Aq, mf * 256, mf, 8, A_x, r_Ax,
                               lambda kd, toff=toff: kd * 512 + toff)
                    rope_chunk(A_q, r_Aq, mf * 256, toff, 128)
                    for hh in (2 * mf, 2 * mf + 1):
                        ev_mm0 = w.op("V", lambda e, hh=hh: e.tensor_copy(
                            m_Q[0:64, hh:2048:16],
                            A_q[(hh % 2) * 64:(hh % 2) * 64 + 64,
                                (hh // 2) * 256:(hh // 2) * 256 + 128]),
                            deps=r_Aq.rdeps() + (r_mQ.wdeps() if hh == 0 else []),
                            sig=("sV" if hh == 15 else None))
                        if hh == 15:
                            r_mQ.produced(ev_mm0)
                            r_Aq.consumed(ev_mm0)
                for mc in range(2):
                    proj_block(mc, A_lat, r_lat, mc * 256, 8 + mc, 8, A_x, r_Ax,
                               lambda kd, toff=toff: kd * 512 + toff)
                for bu in range(4):
                    g, wdeps = consume_att()
                    for ml in range(4):
                        mf = bu * 4 + ml
                        bank = r_banks[mf % 2]
                        pbx = pb[mf % 2]
                        ev_mm = None
                        for kl in range(2):
                            dep = (wdeps + bank.wdeps() + r_lat.rdeps()) if kl == 0 else []
                            ev_mm = w.op("T", lambda e, g=g, kl=kl, ml=ml, s0=(kl == 0),
                                         s1=(kl == 1): e.matmul(
                                pbx[:, 0:256],
                                Wf[:, g * 1024 + kl * 512 + ml * 128:g * 1024 + kl * 512 + ml * 128 + 128],
                                A_lat[:, kl * 256:(kl + 1) * 256], start=s0, stop=s1),
                                deps=dep, sig=("sT" if kl == 1 else None))
                        bank.produced(ev_mm)
                        r_Wf[g].consumed(ev_mm)
                        r_lat.consumed(ev_mm)
                        ev_ev = w.op("A", lambda e, mf=mf: e.activation(
                            A_kv[:, mf * 256:mf * 256 + 256], pbx[:, 0:256], AF.Identity,
                            bias=c_bias[:, 10 + mf:11 + mf]),
                            deps=bank.rdeps() + r_Akv.wdeps() + [ev_const], sig="sA")
                        bank.consumed(ev_ev)
                        r_Akv.produced(ev_ev)
                        rope_chunk(A_kv, r_Akv, mf * 256, toff, 64)
                        w.op("V", lambda e, mf=mf: e.tensor_copy(
                            m_K[0:64, mf:2048:16],
                            A_kv[0:64, mf * 256:mf * 256 + 128]),
                            deps=r_Akv.rdeps() + (r_mK.wdeps() if mf == 0 else []))
                        ev_mv0 = w.op("V", lambda e, mf=mf: e.tensor_copy(
                            m_V[0:64, mf:2048:16],
                            A_kv[64:128, mf * 256:mf * 256 + 128]),
                            deps=(r_mV.wdeps() if mf == 0 else []),
                            sig=("sV" if mf == 15 else None))
                        if mf == 15:
                            r_mK.produced(ev_mv0)
                            r_mV.produced(ev_mv0)
                            r_Akv.consumed(ev_mv0)

                if it == 0 and ha == 0:
                    dump("dbg_lat", A_lat[:, :], r_lat)
                if it == 0 and ha == 0:
                    dump("dbg_q", A_q[:, :], r_Aq)
                    dump("dbg_kv", A_kv[:, :], r_Akv)

                for sl in range(2):
                    so = sl * 128
                    ev_m = None
                    for h in range(16) if sl == 1 else []:
                        w.op("V", lambda e, h=h: e.tensor_copy(
                            m_Q[0:64, h:2048:16],
                            A_q[(h % 2) * 64:(h % 2) * 64 + 64,
                                (h // 2) * 256 + so:(h // 2) * 256 + so + 128]),
                            deps=r_Aq.rdeps() + (r_mQ.wdeps() if h == 0 else []))
                        w.op("V", lambda e, h=h: e.tensor_copy(
                            m_K[0:64, h:2048:16],
                            A_kv[0:64, h * 256 + so:h * 256 + so + 128]),
                            deps=r_Akv.rdeps() + (r_mK.wdeps() if h == 0 else []))
                        ev_m = w.op("V", lambda e, h=h: e.tensor_copy(
                            m_V[0:64, h:2048:16],
                            A_kv[64:128, h * 256 + so:h * 256 + so + 128]),
                            deps=(r_mV.wdeps() if h == 0 else []),
                            sig=("sV" if h == 15 else None))
                    if sl == 1:
                        r_mQ.produced(ev_m)
                        r_mK.produced(ev_m)
                        r_mV.produced(ev_m)
                        r_Aq.consumed(ev_m)
                        r_Akv.consumed(ev_m)
                    if it == 0 and ha == 0 and sl == 0:
                        dump("dbg_mQ", m_Q[:, :], r_mQ)
                        dump("dbg_mK", m_K[:, :], r_mK)
                        dump("dbg_mV", m_V[:, :], r_mV)

                    for g16 in range(16):
                        i2 = g16 % 2
                        bqk = r_banks[2 + i2]
                        w.op("T", lambda e, g16=g16, i2=i2: e.matmul(
                            pb[2 + i2][:, 0:128], m_K[0:64, g16 * 128:(g16 + 1) * 128],
                            m_Q[0:64, g16 * 128:(g16 + 1) * 128], start=True, stop=False),
                            deps=r_mK.rdeps() + r_mQ.rdeps() + bqk.wdeps())
                        ev_qk = w.op("T", lambda e, i2=i2: e.matmul(
                            pb[2 + i2][:, 0:128], c_maskC[:, :], c_maskD[:, :],
                            start=False, stop=True), deps=[ev_const], sig="sT")
                        bqk.produced(ev_qk)
                        r_mK.consumed(ev_qk)
                        r_mQ.consumed(ev_qk)
                        ev_exp = w.op("A", lambda e, i2=i2: e.activation(
                            p_hat[i2][:, :], pb[2 + i2][:, 0:128], AF.Exp, scale=0.125),
                            deps=bqk.rdeps() + r_ph[i2].wdeps(), sig="sA")
                        bqk.consumed(ev_exp)
                        r_ph[i2].produced(ev_exp)
                        if it == 0 and ha == 0 and sl == 0 and g16 == 0:
                            dump("dbg_p", p_hat[0][:, :], r_ph[0])
                        btr = r_banks[6 + i2]
                        ev_tr = w.op("T", lambda e, g16=g16, i2=i2: e.transpose(
                            pb[6 + i2][:, 0:64], m_V[0:64, g16 * 128:(g16 + 1) * 128],
                            c_ident[0:64, 0:64]),
                            deps=r_mV.rdeps() + btr.wdeps() + [ev_const], sig="sT")
                        btr.produced(ev_tr)
                        r_mV.consumed(ev_tr)
                        ev_vh = w.op("A", lambda e, i2=i2: e.activation(
                            v_hat[i2][:, 0:64], pb[6 + i2][:, 0:64], AF.Copy),
                            deps=btr.rdeps() + r_vh[i2].wdeps(), sig="sA")
                        btr.consumed(ev_vh)
                        r_vh[i2].produced(ev_vh)
                        bav = r_banks[4 + i2]
                        ev_av = w.op("T", lambda e, i2=i2: e.matmul(
                            pb[4 + i2][:, 0:65], p_hat[i2][:, :], v_hat[i2][:, 0:65],
                            start=True, stop=True),
                            deps=r_ph[i2].rdeps() + r_vh[i2].rdeps() + bav.wdeps(), sig="sT")
                        bav.produced(ev_av)
                        r_ph[i2].consumed(ev_av)
                        r_vh[i2].consumed(ev_av)
                        ev_rc = w.op("V", lambda e, i2=i2: e.reciprocal(
                            r_rec[:, i2:i2 + 1], pb[4 + i2][:, 64:65]),
                            deps=bav.rdeps() + r_rr[i2].wdeps(), sig="sV")
                        r_rr[i2].produced(ev_rc)
                        ev_ml = w.op("A", lambda e, i2=i2: e.mul(
                            t_av[i2][:, :], pb[4 + i2][:, 0:64], r_rec[:, i2:i2 + 1]),
                            deps=bav.rdeps() + [ev_rc] + r_tav[i2].wdeps(), sig="sA")
                        bav.consumed(ev_ml)
                        r_rr[i2].consumed(ev_ml)
                        r_tav[i2].produced(ev_ml)
                        ev_t2 = w.op("T", lambda e, i2=i2: e.transpose(
                            pb[6 + i2][0:64, 0:128], t_av[i2][:, :], c_ident[:, :]),
                            deps=r_tav[i2].rdeps() + btr.wdeps(), sig="sT")
                        btr.produced(ev_t2)
                        r_tav[i2].consumed(ev_t2)
                        ev_wv = w.op("A", lambda e, g16=g16, i2=i2: e.activation(
                            m_V[0:64, g16 * 128:(g16 + 1) * 128], pb[6 + i2][0:64, 0:128],
                            AF.Copy), deps=btr.rdeps() + r_mV.wdeps(), sig="sA")
                        btr.consumed(ev_wv)
                        r_mV.produced(ev_wv)
                    ev_u = None
                    for h in range(16):
                        ev_u = w.op("V", lambda e, h=h: e.tensor_copy(
                            A_attn[(h % 2) * 64:(h % 2) * 64 + 64,
                                   (h // 2) * 512 + toff + so:(h // 2) * 512 + toff + so + 128],
                            m_V[0:64, h:2048:16]),
                            deps=r_mV.rdeps() + (r_Aattn.wdeps() if h == 0 else []),
                            sig=("sV" if h == 15 else None))
                    r_mV.consumed(ev_u)
                    r_Aattn.produced(ev_u)

            if it == 0:
                dump("dbg_att", A_attn[:, :], r_Aattn)
            for mf in range(8):
                g, wdeps = consume_att()
                bank = r_banks[mf % 2]
                pbx = pb[mf % 2]
                ev_mm = None
                for kd in range(8):
                    dep = (wdeps + bank.wdeps() + r_Aattn.rdeps()) if kd == 0 else []
                    ev_mm = w.op("T", lambda e, g=g, kd=kd, s0=(kd == 0),
                                 s1=(kd == 7): e.matmul(
                        pbx[:, 0:512], Wf[:, g * 1024 + kd * 128:g * 1024 + (kd + 1) * 128],
                        A_attn[:, kd * 512:(kd + 1) * 512], start=s0, stop=s1),
                        deps=dep, sig=("sT" if kd == 7 else None))
                bank.produced(ev_mm)
                r_Wf[g].consumed(ev_mm)
                r_Aattn.consumed(ev_mm)
                ev_ev = w.op("V", lambda e, mf=mf: e.scalar_tensor_tensor(
                    A_kv[:, mf * 512:(mf + 1) * 512], pbx[:, 0:512],
                    c_bias[:, 26 + mf:27 + mf], A_x[:, mf * 512:(mf + 1) * 512],
                    op0=ALU.add, op1=ALU.add),
                    deps=bank.rdeps() + r_Ax.rdeps() + [ev_const]
                    + (r_Akv.wdeps() if mf == 0 else []), sig="sV")
                bank.consumed(ev_ev)
                r_Ax.consumed(ev_ev)
                r_Akv.produced(ev_ev)

            if it == 0:
                dump("dbg_sx", A_kv[:, :], r_Akv)
            layernorm(A_kv, r_Akv, A_x, r_Ax)
            if it == 0:
                dump("dbg_x1", A_x[:, :], r_Ax)
            split(A_x, r_Ax, A_h, r_Ah, A_l, r_Al, 4096)
            bank = r_banks[0]
            ev_mm = None
            for kd in range(8):
                dep = ([ev_const] + bank.wdeps() + r_Ax.rdeps()) if kd == 0 else []
                ev_mm = w.op("T", lambda e, kd=kd, s0=(kd == 0), s1=(kd == 7): e.matmul(
                    pb[0][0:8, 0:512], c_wg_f[:, kd * 8:(kd + 1) * 8],
                    A_x[:, kd * 512:(kd + 1) * 512], start=s0, stop=s1),
                    deps=dep, sig=("sT" if kd == 7 else None))
            bank.produced(ev_mm)
            r_Ax.consumed(ev_mm)
            b6 = r_banks[6]
            b7 = r_banks[7]
            ev_g8 = w.op("A", lambda e: e.activation(
                G8[:, :], pb[0][0:8, 0:512], AF.Copy),
                deps=bank.rdeps() + r_G8.wdeps(), sig="sA")
            bank.consumed(ev_g8)
            r_G8.produced(ev_g8)
            ev_tr = None
            for blk in range(4):
                ev_tr = w.op("T", lambda e, blk=blk: e.transpose(
                    pb[6][:, blk * 8:(blk + 1) * 8], G8[0:8, blk * 128:(blk + 1) * 128],
                    c_ident[0:8, 0:8]),
                    deps=([ev_g8, ev_const] + b6.wdeps()) if blk == 0 else [],
                    sig=("sT" if blk == 3 else None))
            b6.produced(ev_tr)
            r_G8.consumed(ev_tr)
            ev_gt = w.op("A", lambda e: e.activation(Gt[:, :], pb[6][:, 0:32], AF.Copy),
                         deps=b6.rdeps() + r_Gtok.wdeps(), sig="sA")
            b6.consumed(ev_gt)
            if it == 0:
                r_gtd = Res(); r_gtd.produced(ev_gt)
                dump("dbg_gt", Gt[:, :], r_gtd)
            ev_v = ev_gt
            for blk in range(4):
                bs = slice(blk * 8, (blk + 1) * 8)
                ev_v = w.op("V", lambda e, bs=bs, blk=blk: e.reduce_max(
                    Gm[:, blk:blk + 1], Gt[:, bs], axis=mybir.AxisListType.X),
                    deps=[ev_gt], sig="sV", sdeps=[ev_v])
                ev_v = w.op("V", lambda e, bs=bs, blk=blk: e.tensor_scalar(
                    Gwt[:, bs], Gt[:, bs], Gm[:, blk:blk + 1], None, op0=ALU.is_ge),
                    sig="sV", sdeps=[ev_v])
                ev_v = w.op("V", lambda e, bs=bs: e.scalar_tensor_tensor(
                    Gs2t[:, bs], Gwt[:, bs], -1e30, Gt[:, bs], op0=ALU.mult, op1=ALU.add),
                    sig="sV", sdeps=[ev_v])
                ev_v = w.op("V", lambda e, bs=bs, blk=blk: e.reduce_max(
                    Gm[:, 4 + blk:5 + blk], Gs2t[:, bs], axis=mybir.AxisListType.X),
                    sig="sV", sdeps=[ev_v])
                ev_v = w.op("V", lambda e, bs=bs, blk=blk: e.tensor_scalar(
                    Gwt[:, bs], Gt[:, bs], Gm[:, 4 + blk:5 + blk], None, op0=ALU.is_ge),
                    sig="sV", sdeps=[ev_v])
                ev_v = w.op("V", lambda e, bs=bs: e.tensor_tensor(
                    Gs2t[:, bs], Gt[:, bs], Gwt[:, bs], op=ALU.mult), sig="sV", sdeps=[ev_v])
            r_Gtok.produced(ev_v)
            if it == 0:
                r_d1 = Res(); r_d1.produced(ev_v)
                dump("dbg_gm", Gm[:, :], r_d1)
                r_d2 = Res(); r_d2.produced(ev_v)
                dump("dbg_gwt", Gwt[:, :], r_d2)
                r_d3 = Res(); r_d3.produced(ev_v)
                dump("dbg_gs2", Gs2t[:, :], r_Gtok)
            ev_tb = None
            for blk in range(4):
                ev_tb = w.op("T", lambda e, blk=blk: e.transpose(
                    pb[7][0:8, blk * 128:(blk + 1) * 128], Gs2t[:, blk * 8:(blk + 1) * 8],
                    c_ident[:, :]),
                    deps=([ev_v] + b7.wdeps()) if blk == 0 else [],
                    sig=("sT" if blk == 3 else None))
            b7.produced(ev_tb)
            r_Gtok.consumed(ev_tb)
            ev_w8 = w.op("A", lambda e: e.activation(G_w8[:, :], pb[7][0:8, 0:512], AF.Copy),
                         deps=b7.rdeps() + r_Gw8.wdeps(), sig="sA")
            b7.consumed(ev_w8)
            r_Gw8.produced(ev_w8)
            if it == 0:
                dump("dbg_g8", G8[:, :], r_G8)
                dump("dbg_w8", G_w8[:, :], r_Gw8)

            for e_i in range(E):
                b6 = r_banks[6]
                ev_bw = w.op("T", lambda e, e_i=e_i: e.matmul(
                    pb[6][:, :], c_sel[0:8, e_i * 128:(e_i + 1) * 128], G_w8[0:8, 0:512],
                    start=True, stop=True),
                    deps=r_Gw8.rdeps() + [ev_const] + b6.wdeps(), sig="sT")
                b6.produced(ev_bw)
                r_Gw8.consumed(ev_bw)
                ev_wb = w.op("A", lambda e: e.activation(wB_s[:, :], pb[6][:, :], AF.Copy),
                             deps=b6.rdeps() + r_wB.wdeps(), sig="sA")
                b6.consumed(ev_wb)
                r_wB.produced(ev_wb)

                for hb in range(NHB):
                    g, wdeps = consume_unit()
                    bg = r_banks[hb % 2]
                    bu_ = r_banks[2 + hb % 2]
                    ev_g = mm3(
                        pb[hb % 2][:, 0:512],
                        lambda kd, g=g: Wg_h[g][0][:, kd * 128:(kd + 1) * 128],
                        lambda kd, g=g: Wg_l[g][0][:, kd * 128:(kd + 1) * 128],
                        lambda kd: A_h[:, kd * 512:(kd + 1) * 512],
                        lambda kd: A_l[:, kd * 512:(kd + 1) * 512],
                        8, wdeps + bg.wdeps() + r_Ah.rdeps() + r_Al.rdeps())
                    bg.produced(ev_g)
                    ev_u2 = mm3(
                        pb[2 + hb % 2][:, 0:512],
                        lambda kd, g=g: Wg_h[g][1][:, kd * 128:(kd + 1) * 128],
                        lambda kd, g=g: Wg_l[g][1][:, kd * 128:(kd + 1) * 128],
                        lambda kd: A_h[:, kd * 512:(kd + 1) * 512],
                        lambda kd: A_l[:, kd * 512:(kd + 1) * 512],
                        8, bu_.wdeps())
                    bu_.produced(ev_u2)
                    r_W12[g].consumed(ev_u2)
                    r_Ah.consumed(ev_u2)
                    r_Al.consumed(ev_u2)
                    ev_si = w.op("A", lambda e, hb=hb: e.activation(
                        G_s[:, :], pb[hb % 2][:, 0:512], AF.Silu),
                        deps=bg.rdeps() + r_Gs.wdeps(), sig="sA")
                    bg.consumed(ev_si)
                    r_Gs.produced(ev_si)
                    ev_gu = w.op("V", lambda e, hb=hb: e.tensor_tensor(
                        A_lat[:, :], G_s[:, :], pb[2 + hb % 2][:, 0:512], op=ALU.mult),
                        deps=[ev_si] + bu_.rdeps() + r_lat.wdeps(), sig="sV")
                    bu_.consumed(ev_gu)
                    r_Gs.consumed(ev_gu)
                    r_lat.produced(ev_gu)
                    ev_gh = w.op("A", lambda e, hb=hb: e.activation(
                        gu_h[:, hb * 512:(hb + 1) * 512], A_lat[:, :], AF.Copy),
                        deps=[ev_gu] + r_gu.wdeps(), sig="sA")
                    ev_gl = w.op("V", lambda e, hb=hb: e.tensor_tensor(
                        gu_l[:, hb * 512:(hb + 1) * 512], A_lat[:, :],
                        gu_h[:, hb * 512:(hb + 1) * 512], op=ALU.subtract),
                        deps=[ev_gh], sig="sV")
                    r_lat.consumed(ev_gl)
                    r_gu.produced(ev_gl)

                for dc in range(8):
                    by = r_banks[4 + dc % 2]
                    ev_y = None
                    for half in range(2):
                        g3, wdeps3 = consume_unit3()
                        for kh in range(11):
                            hbk = half * 11 + kh
                            s0 = half == 0 and kh == 0
                            is_last = half == 1 and kh == 10
                            dep = (wdeps3 + r_gu.rdeps() + (by.wdeps() if s0 else []))\
                                if kh == 0 else []
                            w.op("T", lambda e, g3=g3, kh=kh, hbk=hbk, s0=s0, dc=dc: e.matmul(
                                pb[4 + dc % 2][:, 0:512], V3_h[g3][:, kh * 128:(kh + 1) * 128],
                                gu_h[:, hbk * 512:(hbk + 1) * 512], start=s0, stop=False),
                                deps=dep)
                            w.op("T", lambda e, g3=g3, kh=kh, hbk=hbk, dc=dc: e.matmul(
                                pb[4 + dc % 2][:, 0:512], V3_h[g3][:, kh * 128:(kh + 1) * 128],
                                gu_l[:, hbk * 512:(hbk + 1) * 512], start=False, stop=False))
                            ev_y = w.op("T", lambda e, g3=g3, kh=kh, hbk=hbk, dc=dc,
                                        is_last=is_last: e.matmul(
                                pb[4 + dc % 2][:, 0:512], V3_l[g3][:, kh * 128:(kh + 1) * 128],
                                gu_h[:, hbk * 512:(hbk + 1) * 512], start=False, stop=is_last),
                                sig=("sT" if kh == 10 else None))
                        r_V3[g3].consumed(ev_y)
                        r_gu.consumed(ev_y)
                    by.produced(ev_y)
                    ev_t = w.op("V", lambda e, dc=dc: e.tensor_tensor(
                        G_s[:, :], pb[4 + dc % 2][:, 0:512], wB_s[:, :], op=ALU.mult),
                        deps=by.rdeps() + r_wB.rdeps() + r_Gs.wdeps(), sig="sV")
                    by.consumed(ev_t)
                    r_wB.consumed(ev_t)
                    r_Gs.produced(ev_t)
                    if e_i == 0:
                        ev_a = w.op("V", lambda e, dc=dc: e.tensor_tensor(
                            A_attn[:, dc * 512:(dc + 1) * 512], G_s[:, :],
                            A_x[:, dc * 512:(dc + 1) * 512], op=ALU.add),
                            deps=[ev_t] + r_Ax.rdeps() + r_Aattn.wdeps(), sig="sV")
                        r_Ax.consumed(ev_a)
                    else:
                        ev_u = w.op("V", lambda e, dc=dc: e.tensor_tensor(
                            A_lat[:, 0:512], A_attn[:, dc * 512:(dc + 1) * 512], G_s[:, :],
                            op=ALU.add),
                            deps=[ev_t] + r_Aattn.rdeps() + r_lat.wdeps(), sig="sV")
                        r_lat.produced(ev_u)
                        ev_a = w.op("V", lambda e, dc=dc: e.tensor_copy(
                            A_attn[:, dc * 512:(dc + 1) * 512], A_lat[:, 0:512]),
                            sig="sV", sdeps=[ev_u])
                        r_lat.consumed(ev_a)
                    r_Gs.consumed(ev_a)
                    r_Aattn.produced(ev_a)

            if it == 0:
                dump("dbg_acc", A_attn[:, :], r_Aattn)
            layernorm(A_attn, r_Aattn, A_x, r_Ax)
            ev_store_prev = dma(lambda e, it=it: e.dma_start(
                out=outT[:, it * 512:(it + 1) * 512].rearrange("(c p) t -> p c t", p=128),
                in_=A_x.ap().rearrange("p (c t) -> p c t", c=8, t=512)),
                deps=r_Ax.rdeps(), sig="dOut")
            r_Ax.consumed(ev_store_prev)

        w.raw_wait("Y", "dOut", w.ctr["dOut"])
        return w

    class _ScalarShim:
        def __init__(self, eng):
            self._e = eng

        def __getattr__(self, k):
            return getattr(self._e, k)

        def tensor_copy(self, out, in_):
            return self._e.activation(out, in_, AF.Copy)

    @block.sync
    def _(eng):
        walk("Y", eng)

    @block.tensor
    def _(eng):
        walk("T", eng)

    @block.scalar
    def _(eng):
        walk("A", _ScalarShim(eng))

    @block.vector
    def _(eng):
        walk("V", eng)

    es.close()
    return nc


def _make_tables(core):
    t0 = (core % 2) * TPC
    inv = (1.0 / 10000.0 ** (np.arange(0, DH, 2, dtype=np.float32) / np.float32(DH))).astype(np.float32)
    pos = np.arange(t0, t0 + TPC, dtype=np.float32)
    fr = (pos[:, None] * inv[None, :]).astype(np.float32)
    sin = np.sin(fr).astype(np.float32)
    cos = np.cos(fr).astype(np.float32)
    p = np.arange(128)
    j = (p % 64) % 32
    sign = np.where((p % 64) < 32, -1.0, 1.0).astype(np.float32)
    C = np.ascontiguousarray(cos[:, j].T)
    S = np.ascontiguousarray(sin[:, j].T * sign[:, None])
    return C, S


def _sel8():
    S = np.zeros((8, 1024), np.float32)
    for e in range(8):
        S[e, e * 128:(e + 1) * 128] = 1.0
    return S


def _mask_cd():
    C = np.zeros((9, 128), np.float32)
    Dm = np.zeros((9, 128), np.float32)
    for t in range(8):
        C[t, 16 * t:16 * t + 16] = 1.0
        Dm[t, 16 * t:16 * t + 16] = BIG
    C[8, :] = 1.0
    Dm[8, :] = -BIG
    return C, Dm


_NC_CACHE = {}


def kernel(L0_attn_x, L0_attn_Wq, L0_attn_bq, L0_attn_Wdown, L0_attn_bdown,
           L0_attn_Wup, L0_attn_bup, L0_attn_Wout, L0_attn_bout,
           L0_ffn_Wg, L0_ffn_W1, L0_ffn_W2, L0_ffn_W3):
    x = np.asarray(L0_attn_x, np.float32)
    xTfull = np.ascontiguousarray(x.reshape(NCORES * TPC, D).T)

    biasp = np.zeros((128, 34), np.float32)
    biasp[:, 0:8] = np.asarray(L0_attn_bq, np.float32).reshape(8, 128).T
    biasp[:, 8:10] = np.asarray(L0_attn_bdown, np.float32).reshape(2, 128).T
    biasp[:, 10:26] = np.asarray(L0_attn_bup, np.float32).reshape(16, 128).T
    biasp[:, 26:34] = np.asarray(L0_attn_bout, np.float32).reshape(8, 128).T
    mC, mD = _mask_cd()

    shared = {
        "wq": np.ascontiguousarray(np.asarray(L0_attn_Wq, np.float32)),
        "wdn": np.ascontiguousarray(np.asarray(L0_attn_Wdown, np.float32)),
        "wup": np.ascontiguousarray(np.asarray(L0_attn_Wup, np.float32)),
        "wout": np.ascontiguousarray(np.asarray(L0_attn_Wout, np.float32)),
        "wgT": np.ascontiguousarray(np.asarray(L0_ffn_Wg, np.float32).T),
        "w1": np.ascontiguousarray(np.asarray(L0_ffn_W1, np.float32).transpose(0, 2, 1)),
        "w2": np.ascontiguousarray(np.asarray(L0_ffn_W2, np.float32).transpose(0, 2, 1)),
        "w3": np.ascontiguousarray(np.asarray(L0_ffn_W3, np.float32).transpose(0, 2, 1)),
        "biasp": biasp, "maskC": mC, "maskD": mD,
        "identD": np.eye(128, dtype=np.float32),
        "onesrow": np.ones((1, 128), np.float32),
        "sel8": _sel8(),
    }

    if "nc" not in _NC_CACHE:
        _NC_CACHE["nc"] = build_nc(NTAU)
    nc = _NC_CACHE["nc"]

    in_maps = []
    for c in range(NCORES):
        C, S = _make_tables(c)
        m = dict(shared)
        m["xT"] = np.ascontiguousarray(xTfull[:, c * TPC:(c + 1) * TPC])
        m["tabC"] = C
        m["tabS"] = S
        in_maps.append(m)

    res = run_bass_kernel_spmd(nc, in_maps, core_ids=list(range(NCORES)))
    outs = [np.asarray(r["outT"]) for r in res.results]
    outT = np.concatenate(outs, axis=1)
    return np.ascontiguousarray(outT.T).reshape(4, 4096, D).astype(np.float32)


# version 14
# speedup vs baseline: 1.0716x; 1.0716x over previous
import numpy as np
from contextlib import ExitStack

import concourse.bass as bass
import concourse.mybir as mybir
from concourse.bass_utils import run_bass_kernel_spmd

F32 = mybir.dt.float32
BF16 = mybir.dt.bfloat16
AF = mybir.ActivationFunctionType
ALU = mybir.AluOpType

D = 1024
NH, DH, DLAT = 16, 64, 256
E, HFF = 8, 2816
NCORES = 8
TPC = 2048
TAU = 512
NTAU = TPC // TAU
NHB = HFF // 128
BIG = 1.0e5
EPS = 1e-5
NB12 = E * NHB * 2
NB3 = E * 8 * 2


class Res:
    __slots__ = ("p", "c")

    def __init__(self):
        self.p = None
        self.c = []

    def rdeps(self):
        return [self.p] if self.p else []

    def wdeps(self):
        d = list(self.c)
        if self.p:
            d.append(self.p)
        return d

    def produced(self, ev):
        if ev is not None:
            self.p = ev
            self.c = []

    def consumed(self, ev):
        if ev is not None:
            self.c.append(ev)


class Walk:

    def __init__(self, role, eng, sems, owner):
        self.role = role
        self.eng = eng
        self.sems = sems
        self.owner = owner
        self.ctr = {k: 0 for k in sems}
        self.last_wait = {k: 0 for k in sems}

    def op(self, engine, emit, deps=(), sig=None, sdeps=()):
        ev = None
        if sig is not None:
            amt = 16 if self.owner[sig] == "dma" else 1
            self.ctr[sig] += amt
            ev = (sig, self.ctr[sig])
        if self.role == engine:
            for d in deps:
                if d is None:
                    continue
                s, v = d
                if v <= 0 or self.owner[s] == self.role:
                    continue
                if v <= self.last_wait[s]:
                    continue
                self.eng.wait_ge(self.sems[s], v)
                self.last_wait[s] = v
            for d in sdeps:
                if d is None:
                    continue
                s, v = d
                if v <= 0 or v <= self.last_wait[s]:
                    continue
                self.eng.wait_ge(self.sems[s], v)
                self.last_wait[s] = v
            inst = emit(self.eng)
            if sig is not None:
                inst.then_inc(self.sems[sig], 16 if self.owner[sig] == "dma" else 1)
        return ev

    def raw_wait(self, engine, sem, val):
        if self.role == engine and val > self.last_wait[sem]:
            self.eng.wait_ge(self.sems[sem], val)
            self.last_wait[sem] = val


def build_nc(ntau=NTAU, debug=False):
    nc = bass.Bass("TRN2", target_bir_lowering=False, detect_race_conditions=False)

    def din(name, shape, dtype=F32):
        return nc.dram_tensor(name, shape, dtype, kind="ExternalInput").ap()

    xT = din("xT", [D, TPC])
    tabC = din("tabC", [128, TPC])
    tabS = din("tabS", [128, TPC])
    wq = din("wq", [D, D])
    wdn = din("wdn", [D, DLAT])
    wup = din("wup", [DLAT, 2 * D])
    wout = din("wout", [D, D])
    wgT = din("wgT", [D, E])
    w1 = din("w1", [E, D, HFF])
    w2 = din("w2", [E, D, HFF])
    w3 = din("w3", [E, HFF, D])
    biasp = din("biasp", [128, 34])
    maskC = din("maskC", [9, 128])
    maskD = din("maskD", [9, 128])
    identD = din("identD", [128, 128])
    onesrow = din("onesrow", [1, 128])
    sel8 = din("sel8", [8, 1024])
    outT = nc.dram_tensor("outT", [D, TPC], F32, kind="ExternalOutput").ap()
    DBG = {}
    if debug:
        for nm, shp in [("dbg_x", [128, 4096]), ("dbg_lat", [128, 512]),
                        ("dbg_q", [128, 2048]), ("dbg_kv", [128, 4096]),
                        ("dbg_mQ", [64, 2048]), ("dbg_mK", [64, 2048]), ("dbg_mV", [64, 2048]),
                        ("dbg_p", [128, 128]), ("dbg_att", [128, 4096]),
                        ("dbg_sx", [128, 4096]), ("dbg_x1", [128, 4096]),
                        ("dbg_g8", [8, 512]), ("dbg_w8", [8, 512]),
                        ("dbg_gt", [128, 32]), ("dbg_gm", [128, 8]),
                        ("dbg_gwt", [128, 32]), ("dbg_gs2", [128, 32]),
                        ("dbg_acc", [128, 4096])]:
            DBG[nm] = nc.dram_tensor(nm, shp, F32, kind="ExternalOutput").ap()

    s12h = nc.dram_tensor("s12h", [NB12, 128, 1024], BF16).ap()
    s12l = nc.dram_tensor("s12l", [NB12, 128, 1024], BF16).ap()
    s3h = nc.dram_tensor("s3h", [NB3, 128, 1408], BF16).ap()
    s3l = nc.dram_tensor("s3l", [NB3, 128, 1408], BF16).ap()

    es = ExitStack()
    SB = es.enter_context

    c_ident = SB(nc.sbuf_tensor("c_ident", [128, 128], F32))
    c_onesrow = SB(nc.sbuf_tensor("c_onesrow", [1, 128], F32))
    c_onescol = SB(nc.sbuf_tensor("c_onescol", [128, 1], F32))
    c_sel = SB(nc.sbuf_tensor("c_sel", [8, 1024], F32))
    c_maskC = SB(nc.sbuf_tensor("c_maskC", [9, 128], F32))
    c_maskD = SB(nc.sbuf_tensor("c_maskD", [9, 128], F32))
    c_bias = SB(nc.sbuf_tensor("c_bias", [128, 34], F32))
    c_wg_f = SB(nc.sbuf_tensor("c_wg_f", [128, 64], F32))
    T_tab = SB(nc.sbuf_tensor("T_tab", [128, 512], F32))
    A_x = SB(nc.sbuf_tensor("A_x", [128, 4096], F32))
    A_attn = SB(nc.sbuf_tensor("A_attn", [128, 4096], F32))
    A_h = SB(nc.sbuf_tensor("A_h", [128, 4096], BF16))
    A_l = SB(nc.sbuf_tensor("A_l", [128, 4096], BF16))
    A_q = SB(nc.sbuf_tensor("A_q", [128, 2048], F32))
    A_kv = SB(nc.sbuf_tensor("A_kv", [128, 4096], F32))
    A_lat = SB(nc.sbuf_tensor("A_lat", [128, 512], F32))
    R_tmp = SB(nc.sbuf_tensor("R_tmp", [128, 256], F32))
    G_s = SB(nc.sbuf_tensor("G_s", [128, 512], F32))
    m_Q = SB(nc.sbuf_tensor("m_Q", [64, 2048], F32))
    m_K = SB(nc.sbuf_tensor("m_K", [64, 2048], F32))
    m_V = SB(nc.sbuf_tensor("m_V", [64, 2048], F32))
    p_hat = [SB(nc.sbuf_tensor(f"p_hat{i}", [128, 128], F32)) for i in range(2)]
    v_hat = [SB(nc.sbuf_tensor(f"v_hat{i}", [128, 65], F32)) for i in range(2)]
    t_av = [SB(nc.sbuf_tensor(f"t_av{i}", [128, 64], F32)) for i in range(2)]
    r_rec = SB(nc.sbuf_tensor("r_rec", [128, 2], F32))
    S_scal = SB(nc.sbuf_tensor("S_scal", [1, 2048], F32))
    G8 = SB(nc.sbuf_tensor("G8", [8, 512], F32))
    Gt = SB(nc.sbuf_tensor("Gt", [128, 32], F32))
    Gwt = SB(nc.sbuf_tensor("Gwt", [128, 32], F32))
    Gs2t = SB(nc.sbuf_tensor("Gs2t", [128, 32], F32))
    Gm = SB(nc.sbuf_tensor("Gm", [128, 8], F32))
    G_w8 = SB(nc.sbuf_tensor("G_w8", [8, 512], F32))
    wB_s = SB(nc.sbuf_tensor("wB_s", [128, 512], F32))
    gu_h = SB(nc.sbuf_tensor("gu_h", [128, NHB * 512], BF16))
    gu_l = SB(nc.sbuf_tensor("gu_l", [128, NHB * 512], BF16))
    Wf = SB(nc.sbuf_tensor("Wf", [128, 2048], F32))
    Wg_h = [[SB(nc.sbuf_tensor(f"Wgh{g}{s}", [128, 1024], BF16)) for s in range(2)] for g in range(2)]
    Wg_l = [[SB(nc.sbuf_tensor(f"Wgl{g}{s}", [128, 1024], BF16)) for s in range(2)] for g in range(2)]
    V3_h = [SB(nc.sbuf_tensor(f"V3h{g}", [128, 1408], BF16)) for g in range(2)]
    V3_l = [SB(nc.sbuf_tensor(f"V3l{g}", [128, 1408], BF16)) for g in range(2)]

    pb = [es.enter_context(nc.psum_tensor(f"pb{i}", [128, 512], F32)) for i in range(8)]

    sem_owner = {
        "dX": "dma", "dC": "dma", "dT": "dma",
        "dWa": "dma", "dWb": "dma", "dVa": "dma", "dVb": "dma",
        "dAa": "dma", "dAb": "dma",
        "dScr": "dma", "dOut": "dma",
        "sT": "T", "sA": "A", "sV": "V",
    }
    sems = {k: es.enter_context(nc.semaphore(k)) for k in sem_owner}
    block = es.enter_context(nc.Block())

    def wq_blk(m):
        return wq[:, m * 128:(m + 1) * 128].rearrange("(kd p) m -> p kd m", p=128)

    def wdn_blk(m):
        return wdn[:, m * 128:(m + 1) * 128].rearrange("(kd p) m -> p kd m", p=128)

    def wup_blk(b):
        return wup[:, b * 512:(b + 1) * 512].rearrange("(kl p) m -> p kl m", p=128)

    def wout_blk(m):
        return wout[:, m * 128:(m + 1) * 128].rearrange("(kd p) m -> p kd m", p=128)

    def w12_blk(t, e, hb):
        src = w1 if t == 0 else w2
        return src[e, :, hb * 128:(hb + 1) * 128].rearrange("(kd p) m -> p kd m", p=128)

    bid = 0
    units = []
    for e in range(E):
        for hb in range(NHB):
            units.append(([(w12_blk(0, e, hb), bid, 0), (w12_blk(1, e, hb), bid + 1, 1)], True))
            bid += 2
    assert bid == NB12, bid
    NUNITS = len(units)

    att_blocks = []
    for _ in range(2):
        for m in range(8):
            att_blocks.append(wq_blk(m))
        for m in range(2):
            att_blocks.append(wdn_blk(m))
        for b in range(4):
            att_blocks.append(wup_blk(b))
    for m in range(8):
        att_blocks.append(wout_blk(m))
    NATT = len(att_blocks)

    def w3_blk(k):
        e_, r = divmod(k, 16)
        dc, half = divmod(r, 2)
        return w3[e_, half * 1408:(half + 1) * 1408, dc * 128:(dc + 1) * 128].rearrange(
            "(kh p) m -> p kh m", p=128)

    def walk(role, eng):
        w = Walk(role, eng, sems, sem_owner)

        r_banks = [Res() for _ in range(8)]
        r_Ax, r_Aattn, r_Ah, r_Al = Res(), Res(), Res(), Res()
        r_Aq, r_Akv, r_lat = Res(), Res(), Res()
        r_tab, r_Rtmp, r_Gs = Res(), Res(), Res()
        r_mQ, r_mK, r_mV = Res(), Res(), Res()
        r_ph = [Res(), Res()]
        r_vh = [Res(), Res()]
        r_tav = [Res(), Res()]
        r_rr = [Res(), Res()]
        r_Ss, r_G8, r_Gtok, r_Gw8, r_wB, r_gu = Res(), Res(), Res(), Res(), Res(), Res()
        r_Wf = [Res(), Res()]
        r_W12 = [Res(), Res()]
        r_V3 = [Res(), Res()]

        def dma(emitfn, deps=(), sig="dX"):
            return w.op("Y", emitfn, deps=deps, sig=sig)

        def dump(nm, buf, res):
            if not debug:
                return
            ev = dma(lambda e: e.dma_start(out=DBG[nm][:, :], in_=buf), deps=res.rdeps(),
                     sig="dOut")
            res.consumed(ev)

        dma(lambda e: e.dma_start(out=c_ident[:, :], in_=identD[:, :]), sig="dC")
        dma(lambda e: e.dma_start(out=c_onesrow[0:1, :], in_=onesrow[0:1, :]), sig="dC")
        dma(lambda e: e.dma_start(out=c_maskC[:, :], in_=maskC[:, :]), sig="dC")
        dma(lambda e: e.dma_start(out=c_maskD[:, :], in_=maskD[:, :]), sig="dC")
        dma(lambda e: e.dma_start(out=c_bias[:, :], in_=biasp[:, :]), sig="dC")
        dma(lambda e: e.dma_start(out=c_sel[:, :], in_=sel8[:, :]), sig="dC")
        ev_const = dma(lambda e: e.dma_start(
            out=c_wg_f.ap().rearrange("p (kd e) -> p kd e", kd=8, e=8),
            in_=wgT.rearrange("(kd p) e -> p kd e", p=128)), sig="dC")
        ev_oc = w.op("V", lambda e: e.memset(c_onescol[:, :], 1.0), sig="sV")
        r_vh[0].produced(w.op("V", lambda e: e.memset(v_hat[0][:, 64:65], 1.0), sig="sV"))
        r_vh[1].produced(w.op("V", lambda e: e.memset(v_hat[1][:, 64:65], 1.0), sig="sV"))

        st = {"next": 0}
        scr_ev = {}

        def load_unit(k):
            g = k % 2
            dsem = "dWa" if g == 0 else "dWb"
            srcs, ws = units[k % NUNITS]
            if k < NUNITS:
                last = None
                for (src_ap, blk, slot) in srcs:
                    fres = r_Wf[slot]
                    fdst = Wf[:, slot * 1024:(slot + 1) * 1024]
                    a, b = src_ap.shape[1], src_ap.shape[2]
                    ev_d = dma(lambda e, src_ap=src_ap, fdst=fdst, a=a, b=b: e.dma_start(
                        out=fdst.rearrange("p (a b) -> p a b", a=a, b=b), in_=src_ap),
                        deps=fres.wdeps(), sig=dsem)
                    fres.produced(ev_d)
                    hdst, ldst = Wg_h[g][slot], Wg_l[g][slot]
                    gres = r_W12[g]
                    ev_h = w.op("A", lambda e, hdst=hdst, fdst=fdst: e.activation(
                        hdst[:, :], fdst, AF.Copy),
                        deps=fres.rdeps() + gres.wdeps(), sig="sA")
                    ev_l = w.op("V", lambda e, ldst=ldst, fdst=fdst, hdst=hdst: e.tensor_tensor(
                        ldst[:, :], fdst, hdst[:, :], op=ALU.subtract),
                        deps=fres.rdeps() + [ev_h], sig="sV")
                    fres.consumed(ev_l)
                    gres.produced(ev_l)
                    if ws:
                        ev_s1 = dma(lambda e, blk=blk, hdst=hdst: e.dma_start(
                            out=s12h[blk], in_=hdst[:, :]), deps=[ev_h], sig="dScr")
                        ev_s2 = dma(lambda e, blk=blk, ldst=ldst: e.dma_start(
                            out=s12l[blk], in_=ldst[:, :]), deps=[ev_l], sig="dScr")
                        gres.consumed(ev_s1)
                        gres.consumed(ev_s2)
                        scr_ev[blk] = ev_s2
                    last = ev_l
                return last
            else:
                gres = r_W12[g]
                deps = gres.wdeps() + [("dScr", w.ctr["dScr"])]
                last = None
                for (src_ap, blk, slot) in srcs:
                    dma(lambda e, blk=blk, g=g, slot=slot: e.dma_start(
                        out=Wg_h[g][slot][:, :], in_=s12h[blk]), deps=deps, sig=dsem)
                    last = dma(lambda e, blk=blk, g=g, slot=slot: e.dma_start(
                        out=Wg_l[g][slot][:, :], in_=s12l[blk]), deps=deps, sig=dsem)
                gres.produced(last)
                return last

        NTOT = NUNITS * ntau

        def consume_unit():
            k = st["next"]
            st["next"] += 1
            if k == 0:
                load_unit(0)
            if st["next"] < NTOT:
                load_unit(st["next"])
            g = k % 2
            return g, r_W12[g].rdeps()

        st3 = {"next": 0}
        scr3_ev = {}

        def load_unit3(k):
            g = k % 2
            dsem = "dVa" if g == 0 else "dVb"
            if k < NB3:
                ev_d = dma(lambda e, k=k: e.dma_start(
                    out=Wf[:, 0:1408].rearrange("p (kh m) -> p kh m", kh=11, m=128),
                    in_=w3_blk(k)),
                    deps=r_Wf[0].wdeps() + r_Wf[1].wdeps(), sig=dsem)
                r_Wf[0].produced(ev_d)
                r_Wf[1].produced(ev_d)
                gres = r_V3[g]
                ev_h = w.op("A", lambda e, g=g: e.activation(V3_h[g][:, :], Wf[:, 0:1408], AF.Copy),
                            deps=[ev_d] + gres.wdeps(), sig="sA")
                ev_l = w.op("V", lambda e, g=g: e.tensor_tensor(
                    V3_l[g][:, :], Wf[:, 0:1408], V3_h[g][:, :], op=ALU.subtract),
                    deps=[ev_d, ev_h], sig="sV")
                r_Wf[0].consumed(ev_l)
                r_Wf[1].consumed(ev_l)
                gres.produced(ev_l)
                ev_s1 = dma(lambda e, k=k, g=g: e.dma_start(out=s3h[k], in_=V3_h[g][:, :]),
                            deps=[ev_h], sig="dScr")
                ev_s2 = dma(lambda e, k=k, g=g: e.dma_start(out=s3l[k], in_=V3_l[g][:, :]),
                            deps=[ev_l], sig="dScr")
                gres.consumed(ev_s1)
                gres.consumed(ev_s2)
                scr3_ev[k] = ev_s2
            else:
                kb = k % NB3
                gres = r_V3[g]
                deps = gres.wdeps() + [("dScr", w.ctr["dScr"])]
                dma(lambda e, kb=kb, g=g: e.dma_start(out=V3_h[g][:, :], in_=s3h[kb]),
                    deps=deps, sig=dsem)
                last = dma(lambda e, kb=kb, g=g: e.dma_start(out=V3_l[g][:, :], in_=s3l[kb]),
                           deps=deps, sig=dsem)
                gres.produced(last)

        N3TOT = NB3 * ntau

        def consume_unit3():
            k = st3["next"]
            st3["next"] += 1
            if k == 0:
                load_unit3(0)
            if st3["next"] < N3TOT:
                load_unit3(st3["next"])
            g = k % 2
            return g, r_V3[g].rdeps()

        ast = {"next": 0}
        NATOT = NATT * ntau

        def load_att(k):
            g = k % 2
            dsem = "dAa" if g == 0 else "dAb"
            blk = att_blocks[k % NATT]
            a_, b_ = blk.shape[1], blk.shape[2]
            ev_d = dma(lambda e, blk=blk, g=g, a_=a_, b_=b_: e.dma_start(
                out=Wf[:, g * 1024:(g + 1) * 1024].rearrange("p (a b) -> p a b", a=a_, b=b_),
                in_=blk), deps=r_Wf[g].wdeps(), sig=dsem)
            r_Wf[g].produced(ev_d)

        def consume_att():
            k = ast["next"]
            ast["next"] += 1
            if k % NATT == 0:
                load_att(k)
            nxt = ast["next"]
            if nxt < NATOT and nxt % NATT != 0:
                load_att(nxt)
            g = k % 2
            return g, r_Wf[g].rdeps()

        def mm3(out_ap, lhs_h_fn, lhs_l_fn, rhs_h_fn, rhs_l_fn, nk, deps_first, sig_last=True,
                start=True, stop=True):
            ev = None
            for kd in range(nk):
                lh, ll = lhs_h_fn(kd), lhs_l_fn(kd)
                rh, rl = rhs_h_fn(kd), rhs_l_fn(kd)
                s0 = start and kd == 0
                dep = deps_first if kd == 0 else []
                w.op("T", lambda e, lh=lh, rh=rh, s0=s0: e.matmul(out_ap, lh, rh, start=s0, stop=False),
                     deps=dep)
                w.op("T", lambda e, lh=lh, rl=rl: e.matmul(out_ap, lh, rl, start=False, stop=False))
                is_last = stop and kd == nk - 1
                ev = w.op("T", lambda e, ll=ll, rh=rh, is_last=is_last: e.matmul(
                    out_ap, ll, rh, start=False, stop=is_last),
                    sig=("sT" if (kd == nk - 1 and sig_last) else None))
            return ev

        def split(src_sb, src_res, h_sb, h_res, l_sb, l_res, ncol, extra_deps=()):
            ev_h = w.op("A", lambda e: e.activation(h_sb[:, 0:ncol], src_sb[:, 0:ncol], AF.Copy),
                        deps=src_res.rdeps() + h_res.wdeps() + list(extra_deps), sig="sA")
            h_res.produced(ev_h)
            ev_l = w.op("V", lambda e: e.tensor_tensor(l_sb[:, 0:ncol], src_sb[:, 0:ncol],
                                                       h_sb[:, 0:ncol], op=ALU.subtract),
                        deps=src_res.rdeps() + [ev_h] + l_res.wdeps(), sig="sV")
            l_res.produced(ev_l)
            src_res.consumed(ev_l)
            h_res.consumed(ev_l)

        def layernorm(src_sb, src_res, out_sb, out_res):
            b6, b7 = r_banks[6], r_banks[7]
            ev_sum = None
            for c in range(8):
                s0 = c == 0
                dep = (src_res.rdeps() + b7.wdeps() + [ev_oc]) if s0 else []
                ev_sum = w.op("T", lambda e, c=c, s0=s0: e.matmul(
                    pb[7][0:1, :], c_onescol[:, 0:1], src_sb[:, c * 512:(c + 1) * 512],
                    start=s0, stop=(c == 7)), deps=dep, sig=("sT" if c == 7 else None))
            b7.produced(ev_sum)
            ev_sq = None
            for c in range(8):
                ev_s = w.op("A", lambda e, c=c: e.activation(
                    G_s[:, :], src_sb[:, c * 512:(c + 1) * 512], AF.Square),
                    deps=src_res.rdeps() + r_Gs.wdeps(), sig="sA")
                r_Gs.produced(ev_s)
                s0 = c == 0
                dep = [ev_s, ev_oc] + (b6.wdeps() if s0 else [])
                ev_sq = w.op("T", lambda e, s0=s0, c=c: e.matmul(
                    pb[6][0:1, :], c_onescol[:, 0:1], G_s[:, :],
                    start=s0, stop=(c == 7)), deps=dep, sig="sT")
                r_Gs.consumed(ev_sq)
            b6.produced(ev_sq)
            SA = S_scal[0:1, 0:512]
            SBb = S_scal[0:1, 512:1024]
            SC = S_scal[0:1, 1024:1536]
            SD = S_scal[0:1, 1536:2048]
            ev = w.op("V", lambda e: e.tensor_single_scalar(
                SA, pb[7][0:1, :], 1.0 / D, op=ALU.mult),
                deps=b7.rdeps() + r_Ss.wdeps(), sig="sV")
            b7.consumed(ev)
            e1 = ev
            ev = w.op("V", lambda e: e.tensor_single_scalar(
                SBb, pb[6][0:1, :], 1.0 / D, op=ALU.mult),
                deps=b6.rdeps(), sig="sV", sdeps=[ev])
            b6.consumed(ev)
            ev = w.op("V", lambda e: e.tensor_tensor(SC, SA, SA, op=ALU.mult),
                      sig="sV", sdeps=[ev])
            ev_mub = w.op("T", lambda e: e.matmul(pb[6][:, :], c_onesrow[0:1, :], SA,
                                                  start=True, stop=True),
                          deps=[ev, ev_const] + b6.wdeps(), sig="sT")
            b6.produced(ev_mub)
            ev = w.op("V", lambda e: e.tensor_tensor(SD, SBb, SC, op=ALU.subtract),
                      sig="sV", sdeps=[ev])
            ev = w.op("V", lambda e: e.tensor_single_scalar(SC, SD, EPS, op=ALU.add),
                      sig="sV", sdeps=[ev])
            ev = w.op("A", lambda e: e.activation(SBb, SC, AF.Sqrt), deps=[ev], sig="sA")
            ev = w.op("V", lambda e: e.reciprocal(SD, SBb), deps=[ev], sig="sV")
            ev = w.op("V", lambda e: e.tensor_tensor(SA, SD, SD, op=ALU.mult),
                      deps=[ev_mub], sig="sV", sdeps=[ev])
            ev = w.op("V", lambda e: e.tensor_tensor(SBb, SA, SC, op=ALU.mult),
                      sig="sV", sdeps=[ev])
            ev = w.op("V", lambda e: e.tensor_scalar(SA, SBb, -0.5, 1.5,
                                                     op0=ALU.mult, op1=ALU.add),
                      sig="sV", sdeps=[ev])
            ev = w.op("V", lambda e: e.tensor_tensor(SBb, SD, SA, op=ALU.mult),
                      sig="sV", sdeps=[ev])
            ev = w.op("V", lambda e: e.tensor_tensor(SA, SBb, SBb, op=ALU.mult),
                      sig="sV", sdeps=[ev])
            ev = w.op("V", lambda e: e.tensor_tensor(SD, SA, SC, op=ALU.mult),
                      sig="sV", sdeps=[ev])
            ev = w.op("V", lambda e: e.tensor_scalar(SA, SD, -0.5, 1.5,
                                                     op0=ALU.mult, op1=ALU.add),
                      sig="sV", sdeps=[ev])
            ev_n = w.op("V", lambda e: e.tensor_tensor(SD, SBb, SA, op=ALU.mult),
                        sig="sV", sdeps=[ev])
            ev_rsb = w.op("T", lambda e: e.matmul(pb[7][:, :], c_onesrow[0:1, :], SD,
                                                  start=True, stop=True),
                          deps=[ev_n] + b7.wdeps(), sig="sT")
            b7.produced(ev_rsb)
            r_Ss.produced(ev_n)
            r_Ss.consumed(ev_rsb)
            r_Ss.consumed(ev_mub)
            ev_last = None
            for c in range(8):
                d1 = w.op("V", lambda e, c=c: e.tensor_tensor(
                    G_s[:, :], src_sb[:, c * 512:(c + 1) * 512],
                    pb[6][:, :], op=ALU.subtract),
                    deps=b6.rdeps() + src_res.rdeps() + r_Gs.wdeps(), sig="sV")
                r_Gs.produced(d1)
                ev_last = w.op("V", lambda e, c=c: e.tensor_tensor(
                    out_sb[:, c * 512:(c + 1) * 512], G_s[:, :],
                    pb[7][:, :], op=ALU.mult),
                    deps=b7.rdeps() + [d1] + (out_res.wdeps() if c == 0 else []), sig="sV")
                r_Gs.consumed(ev_last)
                b6.consumed(ev_last)
                b7.consumed(ev_last)
            src_res.consumed(ev_last)
            out_res.produced(ev_last)

        def rope_chunk(buf, res, col0, tab_off, krows):
            cos = T_tab[:, 0:256]
            sin = T_tab[:, 256:512]
            sw = None
            nblk = krows // 64
            for b in range(nblk):
                w.op("A", lambda e, b=b: e.tensor_copy(
                    R_tmp[b * 64:b * 64 + 32, :], buf[b * 64 + 32:b * 64 + 64, col0:col0 + 256]),
                    deps=res.rdeps() + r_Rtmp.wdeps() + r_tab.rdeps())
                sw = w.op("A", lambda e, b=b: e.tensor_copy(
                    R_tmp[b * 64 + 32:b * 64 + 64, :], buf[b * 64:b * 64 + 32, col0:col0 + 256]),
                    sig="sA")
            r_Rtmp.produced(sw)
            e1 = w.op("V", lambda e: e.tensor_tensor(
                A_lat[0:krows, 0:256], R_tmp[0:krows, :], sin[0:krows, :], op=ALU.mult),
                deps=[sw] + r_tab.rdeps() + r_lat.wdeps(), sig="sV")
            e2 = w.op("V", lambda e: e.tensor_tensor(
                R_tmp[0:krows, :], buf[0:krows, col0:col0 + 256],
                cos[0:krows, :], op=ALU.mult), deps=[sw], sig="sV")
            e3 = w.op("V", lambda e: e.tensor_tensor(
                buf[0:krows, col0:col0 + 256], R_tmp[0:krows, :],
                A_lat[0:krows, 0:256], op=ALU.add), sig="sV")
            r_Rtmp.consumed(e3)
            r_lat.produced(e1)
            r_lat.consumed(e3)
            r_tab.consumed(e3)
            res.produced(e3)


        ev_store_prev = None

        for it in range(ntau):

            xdeps = r_Ax.wdeps() + r_tab.wdeps()
            if ev_store_prev is not None:
                xdeps.append(ev_store_prev)
            dma(lambda e, it=it: e.dma_start(
                out=A_x.ap().rearrange("p (c t) -> p c t", c=8, t=512),
                in_=xT[:, it * 512:(it + 1) * 512].rearrange("(c p) t -> p c t", p=128)),
                deps=xdeps, sig="dX")
            r_Ax.produced(("dX", w.ctr["dX"]))

            if it == 0:
                dump("dbg_x", A_x[:, :], r_Ax)
            split(A_x, r_Ax, A_h, r_Ah, A_l, r_Al, 4096)

            for ha in range(2):
                toff = ha * 256
                dma(lambda e, it=it, ha=ha: e.dma_start(
                    out=T_tab[:, 0:256],
                    in_=tabC[:, it * 512 + ha * 256:it * 512 + ha * 256 + 256]),
                    deps=r_tab.wdeps(), sig="dT")
                ev_tb = dma(lambda e, it=it, ha=ha: e.dma_start(
                    out=T_tab[:, 256:512],
                    in_=tabS[:, it * 512 + ha * 256:it * 512 + ha * 256 + 256]), sig="dT")
                r_tab.produced(ev_tb)

                def proj_block(mf, outbuf, out_res, outcol, bias_col, nk, rhs, rhs_res,
                               rhs_off):
                    g, wdeps = consume_att()
                    bank = r_banks[mf % 2]
                    pbx = pb[mf % 2]
                    ev_mm = None
                    for kd in range(nk):
                        dep = (wdeps + bank.wdeps() + rhs_res.rdeps()) if kd == 0 else []
                        ev_mm = w.op("T", lambda e, g=g, kd=kd, s0=(kd == 0),
                                     s1=(kd == nk - 1): e.matmul(
                            pbx[:, 0:256], Wf[:, g * 1024 + kd * 128:g * 1024 + (kd + 1) * 128],
                            rhs[:, rhs_off(kd):rhs_off(kd) + 256], start=s0, stop=s1),
                            deps=dep, sig=("sT" if kd == nk - 1 else None))
                    bank.produced(ev_mm)
                    r_Wf[g].consumed(ev_mm)
                    rhs_res.consumed(ev_mm)
                    ev_ev = w.op("A", lambda e: e.activation(
                        outbuf[:, outcol:outcol + 256], pbx[:, 0:256], AF.Identity,
                        bias=c_bias[:, bias_col:bias_col + 1]),
                        deps=bank.rdeps() + out_res.wdeps() + [ev_const], sig="sA")
                    bank.consumed(ev_ev)
                    out_res.produced(ev_ev)

                for mf in range(8):
                    proj_block(mf, A_q, r_Aq, mf * 256, mf, 8, A_x, r_Ax,
                               lambda kd, toff=toff: kd * 512 + toff)
                for mc in range(2):
                    proj_block(mc, A_lat, r_lat, mc * 256, 8 + mc, 8, A_x, r_Ax,
                               lambda kd, toff=toff: kd * 512 + toff)
                for bu in range(4):
                    g, wdeps = consume_att()
                    for ml in range(4):
                        mf = bu * 4 + ml
                        bank = r_banks[mf % 2]
                        pbx = pb[mf % 2]
                        ev_mm = None
                        for kl in range(2):
                            dep = (wdeps + bank.wdeps() + r_lat.rdeps()) if kl == 0 else []
                            ev_mm = w.op("T", lambda e, g=g, kl=kl, ml=ml, s0=(kl == 0),
                                         s1=(kl == 1): e.matmul(
                                pbx[:, 0:256],
                                Wf[:, g * 1024 + kl * 512 + ml * 128:g * 1024 + kl * 512 + ml * 128 + 128],
                                A_lat[:, kl * 256:(kl + 1) * 256], start=s0, stop=s1),
                                deps=dep, sig=("sT" if kl == 1 else None))
                        bank.produced(ev_mm)
                        r_Wf[g].consumed(ev_mm)
                        r_lat.consumed(ev_mm)
                        ev_ev = w.op("A", lambda e, mf=mf: e.activation(
                            A_kv[:, mf * 256:mf * 256 + 256], pbx[:, 0:256], AF.Identity,
                            bias=c_bias[:, 10 + mf:11 + mf]),
                            deps=bank.rdeps() + r_Akv.wdeps() + [ev_const], sig="sA")
                        bank.consumed(ev_ev)
                        r_Akv.produced(ev_ev)

                if it == 0 and ha == 0:
                    dump("dbg_lat", A_lat[:, :], r_lat)
                for c in range(8):
                    rope_chunk(A_q, r_Aq, c * 256, toff, 128)
                for g16 in range(16):
                    rope_chunk(A_kv, r_Akv, g16 * 256, toff, 64)
                if it == 0 and ha == 0:
                    dump("dbg_q", A_q[:, :], r_Aq)
                    dump("dbg_kv", A_kv[:, :], r_Akv)

                for sl in range(2):
                    so = sl * 128
                    ev_m = None
                    ev_ma = None
                    for h in range(16):
                        ev_m = w.op("V", lambda e, h=h: e.tensor_copy(
                            m_Q[0:64, h:2048:16],
                            A_q[(h % 2) * 64:(h % 2) * 64 + 64,
                                (h // 2) * 256 + so:(h // 2) * 256 + so + 128]),
                            deps=r_Aq.rdeps() + (r_mQ.wdeps() if h == 0 else []),
                            sig=("sV" if h == 15 else None))
                        w.op("A", lambda e, h=h: e.tensor_copy(
                            m_K[0:64, h:2048:16],
                            A_kv[0:64, h * 256 + so:h * 256 + so + 128]),
                            deps=r_Akv.rdeps() + (r_mK.wdeps() if h == 0 else []))
                        ev_ma = w.op("A", lambda e, h=h: e.tensor_copy(
                            m_V[0:64, h:2048:16],
                            A_kv[64:128, h * 256 + so:h * 256 + so + 128]),
                            deps=(r_mV.wdeps() if h == 0 else []),
                            sig=("sA" if h == 15 else None))
                    r_mQ.produced(ev_m)
                    r_mK.produced(ev_ma)
                    r_mV.produced(ev_ma)
                    r_Aq.consumed(ev_m)
                    r_Akv.consumed(ev_ma)
                    if it == 0 and ha == 0 and sl == 0:
                        dump("dbg_mQ", m_Q[:, :], r_mQ)
                        dump("dbg_mK", m_K[:, :], r_mK)
                        dump("dbg_mV", m_V[:, :], r_mV)

                    for g16 in range(16):
                        i2 = g16 % 2
                        bqk = r_banks[2 + i2]
                        w.op("T", lambda e, g16=g16, i2=i2: e.matmul(
                            pb[2 + i2][:, 0:128], m_K[0:64, g16 * 128:(g16 + 1) * 128],
                            m_Q[0:64, g16 * 128:(g16 + 1) * 128], start=True, stop=False),
                            deps=r_mK.rdeps() + r_mQ.rdeps() + bqk.wdeps())
                        ev_qk = w.op("T", lambda e, i2=i2: e.matmul(
                            pb[2 + i2][:, 0:128], c_maskC[:, :], c_maskD[:, :],
                            start=False, stop=True), deps=[ev_const], sig="sT")
                        bqk.produced(ev_qk)
                        r_mK.consumed(ev_qk)
                        r_mQ.consumed(ev_qk)
                        ev_exp = w.op("A", lambda e, i2=i2: e.activation(
                            p_hat[i2][:, :], pb[2 + i2][:, 0:128], AF.Exp, scale=0.125),
                            deps=bqk.rdeps() + r_ph[i2].wdeps(), sig="sA")
                        bqk.consumed(ev_exp)
                        r_ph[i2].produced(ev_exp)
                        if it == 0 and ha == 0 and sl == 0 and g16 == 0:
                            dump("dbg_p", p_hat[0][:, :], r_ph[0])
                        btr = r_banks[6 + i2]
                        ev_tr = w.op("T", lambda e, g16=g16, i2=i2: e.transpose(
                            pb[6 + i2][:, 0:64], m_V[0:64, g16 * 128:(g16 + 1) * 128],
                            c_ident[0:64, 0:64]),
                            deps=r_mV.rdeps() + btr.wdeps() + [ev_const], sig="sT")
                        btr.produced(ev_tr)
                        r_mV.consumed(ev_tr)
                        ev_vh = w.op("A", lambda e, i2=i2: e.activation(
                            v_hat[i2][:, 0:64], pb[6 + i2][:, 0:64], AF.Copy),
                            deps=btr.rdeps() + r_vh[i2].wdeps(), sig="sA")
                        btr.consumed(ev_vh)
                        r_vh[i2].produced(ev_vh)
                        bav = r_banks[4 + i2]
                        ev_av = w.op("T", lambda e, i2=i2: e.matmul(
                            pb[4 + i2][:, 0:65], p_hat[i2][:, :], v_hat[i2][:, 0:65],
                            start=True, stop=True),
                            deps=r_ph[i2].rdeps() + r_vh[i2].rdeps() + bav.wdeps(), sig="sT")
                        bav.produced(ev_av)
                        r_ph[i2].consumed(ev_av)
                        r_vh[i2].consumed(ev_av)
                        ev_rc = w.op("V", lambda e, i2=i2: e.reciprocal(
                            r_rec[:, i2:i2 + 1], pb[4 + i2][:, 64:65]),
                            deps=bav.rdeps() + r_rr[i2].wdeps(), sig="sV")
                        r_rr[i2].produced(ev_rc)
                        ev_ml = w.op("A", lambda e, i2=i2: e.mul(
                            t_av[i2][:, :], pb[4 + i2][:, 0:64], r_rec[:, i2:i2 + 1]),
                            deps=bav.rdeps() + [ev_rc] + r_tav[i2].wdeps(), sig="sA")
                        bav.consumed(ev_ml)
                        r_rr[i2].consumed(ev_ml)
                        r_tav[i2].produced(ev_ml)
                        ev_t2 = w.op("T", lambda e, i2=i2: e.transpose(
                            pb[6 + i2][0:64, 0:128], t_av[i2][:, :], c_ident[:, :]),
                            deps=r_tav[i2].rdeps() + btr.wdeps(), sig="sT")
                        btr.produced(ev_t2)
                        r_tav[i2].consumed(ev_t2)
                        ev_wv = w.op("A", lambda e, g16=g16, i2=i2: e.activation(
                            m_V[0:64, g16 * 128:(g16 + 1) * 128], pb[6 + i2][0:64, 0:128],
                            AF.Copy), deps=btr.rdeps() + r_mV.wdeps(), sig="sA")
                        btr.consumed(ev_wv)
                        r_mV.produced(ev_wv)
                    ev_u = None
                    for h in range(16):
                        ev_u = w.op("V", lambda e, h=h: e.tensor_copy(
                            A_attn[(h % 2) * 64:(h % 2) * 64 + 64,
                                   (h // 2) * 512 + toff + so:(h // 2) * 512 + toff + so + 128],
                            m_V[0:64, h:2048:16]),
                            deps=r_mV.rdeps() + (r_Aattn.wdeps() if h == 0 else []),
                            sig=("sV" if h == 15 else None))
                    r_mV.consumed(ev_u)
                    r_Aattn.produced(ev_u)

            if it == 0:
                dump("dbg_att", A_attn[:, :], r_Aattn)
            for mf in range(8):
                g, wdeps = consume_att()
                bank = r_banks[mf % 2]
                pbx = pb[mf % 2]
                ev_mm = None
                for kd in range(8):
                    dep = (wdeps + bank.wdeps() + r_Aattn.rdeps()) if kd == 0 else []
                    ev_mm = w.op("T", lambda e, g=g, kd=kd, s0=(kd == 0),
                                 s1=(kd == 7): e.matmul(
                        pbx[:, 0:512], Wf[:, g * 1024 + kd * 128:g * 1024 + (kd + 1) * 128],
                        A_attn[:, kd * 512:(kd + 1) * 512], start=s0, stop=s1),
                        deps=dep, sig=("sT" if kd == 7 else None))
                bank.produced(ev_mm)
                r_Wf[g].consumed(ev_mm)
                r_Aattn.consumed(ev_mm)
                ev_ev = w.op("V", lambda e, mf=mf: e.scalar_tensor_tensor(
                    A_kv[:, mf * 512:(mf + 1) * 512], pbx[:, 0:512],
                    c_bias[:, 26 + mf:27 + mf], A_x[:, mf * 512:(mf + 1) * 512],
                    op0=ALU.add, op1=ALU.add),
                    deps=bank.rdeps() + r_Ax.rdeps() + [ev_const]
                    + (r_Akv.wdeps() if mf == 0 else []), sig="sV")
                bank.consumed(ev_ev)
                r_Ax.consumed(ev_ev)
                r_Akv.produced(ev_ev)

            if it == 0:
                dump("dbg_sx", A_kv[:, :], r_Akv)
            layernorm(A_kv, r_Akv, A_x, r_Ax)
            if it == 0:
                dump("dbg_x1", A_x[:, :], r_Ax)
            split(A_x, r_Ax, A_h, r_Ah, A_l, r_Al, 4096)
            bank = r_banks[0]
            ev_mm = None
            for kd in range(8):
                dep = ([ev_const] + bank.wdeps() + r_Ax.rdeps()) if kd == 0 else []
                ev_mm = w.op("T", lambda e, kd=kd, s0=(kd == 0), s1=(kd == 7): e.matmul(
                    pb[0][0:8, 0:512], c_wg_f[:, kd * 8:(kd + 1) * 8],
                    A_x[:, kd * 512:(kd + 1) * 512], start=s0, stop=s1),
                    deps=dep, sig=("sT" if kd == 7 else None))
            bank.produced(ev_mm)
            r_Ax.consumed(ev_mm)
            b6 = r_banks[6]
            b7 = r_banks[7]
            ev_g8 = w.op("A", lambda e: e.activation(
                G8[:, :], pb[0][0:8, 0:512], AF.Copy),
                deps=bank.rdeps() + r_G8.wdeps(), sig="sA")
            bank.consumed(ev_g8)
            r_G8.produced(ev_g8)
            ev_tr = None
            for blk in range(4):
                ev_tr = w.op("T", lambda e, blk=blk: e.transpose(
                    pb[6][:, blk * 8:(blk + 1) * 8], G8[0:8, blk * 128:(blk + 1) * 128],
                    c_ident[0:8, 0:8]),
                    deps=([ev_g8, ev_const] + b6.wdeps()) if blk == 0 else [],
                    sig=("sT" if blk == 3 else None))
            b6.produced(ev_tr)
            r_G8.consumed(ev_tr)
            ev_gt = w.op("A", lambda e: e.activation(Gt[:, :], pb[6][:, 0:32], AF.Copy),
                         deps=b6.rdeps() + r_Gtok.wdeps(), sig="sA")
            b6.consumed(ev_gt)
            if it == 0:
                r_gtd = Res(); r_gtd.produced(ev_gt)
                dump("dbg_gt", Gt[:, :], r_gtd)
            ev_v = ev_gt
            for blk in range(4):
                bs = slice(blk * 8, (blk + 1) * 8)
                ev_v = w.op("V", lambda e, bs=bs, blk=blk: e.reduce_max(
                    Gm[:, blk:blk + 1], Gt[:, bs], axis=mybir.AxisListType.X),
                    deps=[ev_gt], sig="sV", sdeps=[ev_v])
                ev_v = w.op("V", lambda e, bs=bs, blk=blk: e.tensor_scalar(
                    Gwt[:, bs], Gt[:, bs], Gm[:, blk:blk + 1], None, op0=ALU.is_ge),
                    sig="sV", sdeps=[ev_v])
                ev_v = w.op("V", lambda e, bs=bs: e.scalar_tensor_tensor(
                    Gs2t[:, bs], Gwt[:, bs], -1e30, Gt[:, bs], op0=ALU.mult, op1=ALU.add),
                    sig="sV", sdeps=[ev_v])
                ev_v = w.op("V", lambda e, bs=bs, blk=blk: e.reduce_max(
                    Gm[:, 4 + blk:5 + blk], Gs2t[:, bs], axis=mybir.AxisListType.X),
                    sig="sV", sdeps=[ev_v])
                ev_v = w.op("V", lambda e, bs=bs, blk=blk: e.tensor_scalar(
                    Gwt[:, bs], Gt[:, bs], Gm[:, 4 + blk:5 + blk], None, op0=ALU.is_ge),
                    sig="sV", sdeps=[ev_v])
                ev_v = w.op("V", lambda e, bs=bs: e.tensor_tensor(
                    Gs2t[:, bs], Gt[:, bs], Gwt[:, bs], op=ALU.mult), sig="sV", sdeps=[ev_v])
            r_Gtok.produced(ev_v)
            if it == 0:
                r_d1 = Res(); r_d1.produced(ev_v)
                dump("dbg_gm", Gm[:, :], r_d1)
                r_d2 = Res(); r_d2.produced(ev_v)
                dump("dbg_gwt", Gwt[:, :], r_d2)
                r_d3 = Res(); r_d3.produced(ev_v)
                dump("dbg_gs2", Gs2t[:, :], r_Gtok)
            ev_tb = None
            for blk in range(4):
                ev_tb = w.op("T", lambda e, blk=blk: e.transpose(
                    pb[7][0:8, blk * 128:(blk + 1) * 128], Gs2t[:, blk * 8:(blk + 1) * 8],
                    c_ident[:, :]),
                    deps=([ev_v] + b7.wdeps()) if blk == 0 else [],
                    sig=("sT" if blk == 3 else None))
            b7.produced(ev_tb)
            r_Gtok.consumed(ev_tb)
            ev_w8 = w.op("A", lambda e: e.activation(G_w8[:, :], pb[7][0:8, 0:512], AF.Copy),
                         deps=b7.rdeps() + r_Gw8.wdeps(), sig="sA")
            b7.consumed(ev_w8)
            r_Gw8.produced(ev_w8)
            if it == 0:
                dump("dbg_g8", G8[:, :], r_G8)
                dump("dbg_w8", G_w8[:, :], r_Gw8)

            for e_i in range(E):
                b6 = r_banks[6]
                ev_bw = w.op("T", lambda e, e_i=e_i: e.matmul(
                    pb[6][:, :], c_sel[0:8, e_i * 128:(e_i + 1) * 128], G_w8[0:8, 0:512],
                    start=True, stop=True),
                    deps=r_Gw8.rdeps() + [ev_const] + b6.wdeps(), sig="sT")
                b6.produced(ev_bw)
                r_Gw8.consumed(ev_bw)
                ev_wb = w.op("A", lambda e: e.activation(wB_s[:, :], pb[6][:, :], AF.Copy),
                             deps=b6.rdeps() + r_wB.wdeps(), sig="sA")
                b6.consumed(ev_wb)
                r_wB.produced(ev_wb)

                for hb in range(NHB):
                    g, wdeps = consume_unit()
                    bg = r_banks[hb % 2]
                    bu_ = r_banks[2 + hb % 2]
                    ev_g = mm3(
                        pb[hb % 2][:, 0:512],
                        lambda kd, g=g: Wg_h[g][0][:, kd * 128:(kd + 1) * 128],
                        lambda kd, g=g: Wg_l[g][0][:, kd * 128:(kd + 1) * 128],
                        lambda kd: A_h[:, kd * 512:(kd + 1) * 512],
                        lambda kd: A_l[:, kd * 512:(kd + 1) * 512],
                        8, wdeps + bg.wdeps() + r_Ah.rdeps() + r_Al.rdeps())
                    bg.produced(ev_g)
                    ev_u2 = mm3(
                        pb[2 + hb % 2][:, 0:512],
                        lambda kd, g=g: Wg_h[g][1][:, kd * 128:(kd + 1) * 128],
                        lambda kd, g=g: Wg_l[g][1][:, kd * 128:(kd + 1) * 128],
                        lambda kd: A_h[:, kd * 512:(kd + 1) * 512],
                        lambda kd: A_l[:, kd * 512:(kd + 1) * 512],
                        8, bu_.wdeps())
                    bu_.produced(ev_u2)
                    r_W12[g].consumed(ev_u2)
                    r_Ah.consumed(ev_u2)
                    r_Al.consumed(ev_u2)
                    ev_si = w.op("A", lambda e, hb=hb: e.activation(
                        G_s[:, :], pb[hb % 2][:, 0:512], AF.Silu),
                        deps=bg.rdeps() + r_Gs.wdeps(), sig="sA")
                    bg.consumed(ev_si)
                    r_Gs.produced(ev_si)
                    ev_gu = w.op("V", lambda e, hb=hb: e.tensor_tensor(
                        A_lat[:, :], G_s[:, :], pb[2 + hb % 2][:, 0:512], op=ALU.mult),
                        deps=[ev_si] + bu_.rdeps() + r_lat.wdeps(), sig="sV")
                    bu_.consumed(ev_gu)
                    r_Gs.consumed(ev_gu)
                    r_lat.produced(ev_gu)
                    ev_gh = w.op("A", lambda e, hb=hb: e.activation(
                        gu_h[:, hb * 512:(hb + 1) * 512], A_lat[:, :], AF.Copy),
                        deps=[ev_gu] + r_gu.wdeps(), sig="sA")
                    ev_gl = w.op("V", lambda e, hb=hb: e.tensor_tensor(
                        gu_l[:, hb * 512:(hb + 1) * 512], A_lat[:, :],
                        gu_h[:, hb * 512:(hb + 1) * 512], op=ALU.subtract),
                        deps=[ev_gh], sig="sV")
                    r_lat.consumed(ev_gl)
                    r_gu.produced(ev_gl)

                for dc in range(8):
                    by = r_banks[4 + dc % 2]
                    ev_y = None
                    for half in range(2):
                        g3, wdeps3 = consume_unit3()
                        for kh in range(11):
                            hbk = half * 11 + kh
                            s0 = half == 0 and kh == 0
                            is_last = half == 1 and kh == 10
                            dep = (wdeps3 + r_gu.rdeps() + (by.wdeps() if s0 else []))\
                                if kh == 0 else []
                            w.op("T", lambda e, g3=g3, kh=kh, hbk=hbk, s0=s0, dc=dc: e.matmul(
                                pb[4 + dc % 2][:, 0:512], V3_h[g3][:, kh * 128:(kh + 1) * 128],
                                gu_h[:, hbk * 512:(hbk + 1) * 512], start=s0, stop=False),
                                deps=dep)
                            w.op("T", lambda e, g3=g3, kh=kh, hbk=hbk, dc=dc: e.matmul(
                                pb[4 + dc % 2][:, 0:512], V3_h[g3][:, kh * 128:(kh + 1) * 128],
                                gu_l[:, hbk * 512:(hbk + 1) * 512], start=False, stop=False))
                            ev_y = w.op("T", lambda e, g3=g3, kh=kh, hbk=hbk, dc=dc,
                                        is_last=is_last: e.matmul(
                                pb[4 + dc % 2][:, 0:512], V3_l[g3][:, kh * 128:(kh + 1) * 128],
                                gu_h[:, hbk * 512:(hbk + 1) * 512], start=False, stop=is_last),
                                sig=("sT" if kh == 10 else None))
                        r_V3[g3].consumed(ev_y)
                        r_gu.consumed(ev_y)
                    by.produced(ev_y)
                    ev_t = w.op("V", lambda e, dc=dc: e.tensor_tensor(
                        G_s[:, :], pb[4 + dc % 2][:, 0:512], wB_s[:, :], op=ALU.mult),
                        deps=by.rdeps() + r_wB.rdeps() + r_Gs.wdeps(), sig="sV")
                    by.consumed(ev_t)
                    r_wB.consumed(ev_t)
                    r_Gs.produced(ev_t)
                    if e_i == 0:
                        ev_a = w.op("V", lambda e, dc=dc: e.tensor_tensor(
                            A_attn[:, dc * 512:(dc + 1) * 512], G_s[:, :],
                            A_x[:, dc * 512:(dc + 1) * 512], op=ALU.add),
                            deps=[ev_t] + r_Ax.rdeps() + r_Aattn.wdeps(), sig="sV")
                        r_Ax.consumed(ev_a)
                    else:
                        ev_u = w.op("V", lambda e, dc=dc: e.tensor_tensor(
                            A_lat[:, 0:512], A_attn[:, dc * 512:(dc + 1) * 512], G_s[:, :],
                            op=ALU.add),
                            deps=[ev_t] + r_Aattn.rdeps() + r_lat.wdeps(), sig="sV")
                        r_lat.produced(ev_u)
                        ev_a = w.op("V", lambda e, dc=dc: e.tensor_copy(
                            A_attn[:, dc * 512:(dc + 1) * 512], A_lat[:, 0:512]),
                            sig="sV", sdeps=[ev_u])
                        r_lat.consumed(ev_a)
                    r_Gs.consumed(ev_a)
                    r_Aattn.produced(ev_a)

            if it == 0:
                dump("dbg_acc", A_attn[:, :], r_Aattn)
            layernorm(A_attn, r_Aattn, A_x, r_Ax)
            ev_store_prev = dma(lambda e, it=it: e.dma_start(
                out=outT[:, it * 512:(it + 1) * 512].rearrange("(c p) t -> p c t", p=128),
                in_=A_x.ap().rearrange("p (c t) -> p c t", c=8, t=512)),
                deps=r_Ax.rdeps(), sig="dOut")
            r_Ax.consumed(ev_store_prev)

        w.raw_wait("Y", "dOut", w.ctr["dOut"])
        return w

    class _ScalarShim:
        def __init__(self, eng):
            self._e = eng

        def __getattr__(self, k):
            return getattr(self._e, k)

        def tensor_copy(self, out, in_):
            return self._e.activation(out, in_, AF.Copy)

    @block.sync
    def _(eng):
        walk("Y", eng)

    @block.tensor
    def _(eng):
        walk("T", eng)

    @block.scalar
    def _(eng):
        walk("A", _ScalarShim(eng))

    @block.vector
    def _(eng):
        walk("V", eng)

    es.close()
    return nc


def _make_tables(core):
    t0 = (core % 2) * TPC
    inv = (1.0 / 10000.0 ** (np.arange(0, DH, 2, dtype=np.float32) / np.float32(DH))).astype(np.float32)
    pos = np.arange(t0, t0 + TPC, dtype=np.float32)
    fr = (pos[:, None] * inv[None, :]).astype(np.float32)
    sin = np.sin(fr).astype(np.float32)
    cos = np.cos(fr).astype(np.float32)
    p = np.arange(128)
    j = (p % 64) % 32
    sign = np.where((p % 64) < 32, -1.0, 1.0).astype(np.float32)
    C = np.ascontiguousarray(cos[:, j].T)
    S = np.ascontiguousarray(sin[:, j].T * sign[:, None])
    return C, S


def _sel8():
    S = np.zeros((8, 1024), np.float32)
    for e in range(8):
        S[e, e * 128:(e + 1) * 128] = 1.0
    return S


def _mask_cd():
    C = np.zeros((9, 128), np.float32)
    Dm = np.zeros((9, 128), np.float32)
    for t in range(8):
        C[t, 16 * t:16 * t + 16] = 1.0
        Dm[t, 16 * t:16 * t + 16] = BIG
    C[8, :] = 1.0
    Dm[8, :] = -BIG
    return C, Dm


_NC_CACHE = {}


def kernel(L0_attn_x, L0_attn_Wq, L0_attn_bq, L0_attn_Wdown, L0_attn_bdown,
           L0_attn_Wup, L0_attn_bup, L0_attn_Wout, L0_attn_bout,
           L0_ffn_Wg, L0_ffn_W1, L0_ffn_W2, L0_ffn_W3):
    x = np.asarray(L0_attn_x, np.float32)
    xTfull = np.ascontiguousarray(x.reshape(NCORES * TPC, D).T)

    biasp = np.zeros((128, 34), np.float32)
    biasp[:, 0:8] = np.asarray(L0_attn_bq, np.float32).reshape(8, 128).T
    biasp[:, 8:10] = np.asarray(L0_attn_bdown, np.float32).reshape(2, 128).T
    biasp[:, 10:26] = np.asarray(L0_attn_bup, np.float32).reshape(16, 128).T
    biasp[:, 26:34] = np.asarray(L0_attn_bout, np.float32).reshape(8, 128).T
    mC, mD = _mask_cd()

    shared = {
        "wq": np.ascontiguousarray(np.asarray(L0_attn_Wq, np.float32)),
        "wdn": np.ascontiguousarray(np.asarray(L0_attn_Wdown, np.float32)),
        "wup": np.ascontiguousarray(np.asarray(L0_attn_Wup, np.float32)),
        "wout": np.ascontiguousarray(np.asarray(L0_attn_Wout, np.float32)),
        "wgT": np.ascontiguousarray(np.asarray(L0_ffn_Wg, np.float32).T),
        "w1": np.ascontiguousarray(np.asarray(L0_ffn_W1, np.float32).transpose(0, 2, 1)),
        "w2": np.ascontiguousarray(np.asarray(L0_ffn_W2, np.float32).transpose(0, 2, 1)),
        "w3": np.ascontiguousarray(np.asarray(L0_ffn_W3, np.float32).transpose(0, 2, 1)),
        "biasp": biasp, "maskC": mC, "maskD": mD,
        "identD": np.eye(128, dtype=np.float32),
        "onesrow": np.ones((1, 128), np.float32),
        "sel8": _sel8(),
    }

    if "nc" not in _NC_CACHE:
        _NC_CACHE["nc"] = build_nc(NTAU)
    nc = _NC_CACHE["nc"]

    in_maps = []
    for c in range(NCORES):
        C, S = _make_tables(c)
        m = dict(shared)
        m["xT"] = np.ascontiguousarray(xTfull[:, c * TPC:(c + 1) * TPC])
        m["tabC"] = C
        m["tabS"] = S
        in_maps.append(m)

    res = run_bass_kernel_spmd(nc, in_maps, core_ids=list(range(NCORES)))
    outs = [np.asarray(r["outT"]) for r in res.results]
    outT = np.concatenate(outs, axis=1)
    return np.ascontiguousarray(outT.T).reshape(4, 4096, D).astype(np.float32)
